# version 04
# baseline (speedup 1.0000x reference)
import numpy as np
import ml_dtypes
import concourse.bass as bass
import concourse.mybir as mybir
from concourse.bass_utils import run_bass_kernel_spmd

F32 = mybir.dt.float32
F16 = mybir.dt.float16
BF16 = mybir.dt.bfloat16
U32 = mybir.dt.uint32
U8 = mybir.dt.uint8
AF = mybir.ActivationFunctionType
ALU = mybir.AluOpType
AX = mybir.AxisListType
IOA = bass.IndirectOffsetOnAxis
NPBF = ml_dtypes.bfloat16

ROWW = 132
OROW = 68


def make_cfg(S=16384, NC=8, NB=2, dev_taps=False):
    return dict(S=S, D=1024, DH=64, NR=4,
                NBK=2 ** ((2 * (S // 64)).bit_length() - 1) // 2,
                G=S // 128, NC=NC, TOK=S // NC, NB=NB, dev_taps=dev_taps)


class EK:
    ENGINES = ("sync", "scalar", "tensor", "vector", "gpsimd")

    def __init__(self, nc):
        self.nc = nc
        self.prog = {e: [] for e in self.ENGINES}
        self.ctx = []
        self.sems = {}
        self.lastwait = {e: {} for e in self.ENGINES}
        for e in self.ENGINES:
            self.sems[e] = self._sem("sem_" + e)
        self.dmasems = {}
        self.bar = self._sem("bar")

    def _sem(self, name):
        h = self.nc.semaphore(name)
        s = h.__enter__()
        self.ctx.append(h)
        return Sem(s, name)

    def wait(self, e, tok):
        if tok is None:
            return
        sem, val = tok
        if val <= 0:
            return
        lw = self.lastwait[e]
        if lw.get(sem.name, 0) >= val:
            return
        lw[sem.name] = val
        self.prog[e].append(lambda eng, s=sem, v=val: eng.wait_ge(s.h, v))

    def op(self, e, fn, deps=(), n=1):
        for d in deps:
            self.wait(e, d)
        s = self.sems[e]
        s.val += n
        self.prog[e].append(lambda eng, fn=fn, s=s, n=n: fn(eng).then_inc(s.h, n))
        return (s, s.val)

    def opq(self, e, fn, deps=()):
        for d in deps:
            self.wait(e, d)
        self.prog[e].append(fn)
        return None

    def dmach(self, e, ch):
        key = (e, ch)
        if key not in self.dmasems:
            self.dmasems[key] = self._sem(f"dma_{e}_{ch}")
        return self.dmasems[key]

    def dma(self, e, out, in_, ch="c", deps=(), **kw):
        for d in deps:
            self.wait(e, d)
        s = self.dmach(e, ch)
        s.val += 16
        self.prog[e].append(
            lambda eng, o=out, i=in_, s=s, kw=kw: eng.dma_start(out=o, in_=i, **kw).then_inc(s.h, 16))
        return (s, s.val)

    def idma(self, out, in_, ch, deps=(), **kw):
        for d in deps:
            self.wait("gpsimd", d)
        s = self.dmach("gpsimd", ch)
        s.val += 16
        self.prog["gpsimd"].append(
            lambda eng, o=out, i=in_, s=s, kw=kw: eng.indirect_dma_start(
                out=o, out_offset=kw.get("out_offset"), in_=i,
                in_offset=kw.get("in_offset"),
                element_offset=kw.get("element_offset", 0)).then_inc(s.h, 16))
        return (s, s.val)

    def barrier(self):
        for (e, ch), s in self.dmasems.items():
            self.wait(e, (s, s.val))
        self.bar.val += 5
        t = self.bar.val
        for e in self.ENGINES:
            self.prog[e].append(lambda eng, s=self.bar, t=t: eng.nop().then_inc(s.h, 1))
            self.prog[e].append(lambda eng, s=self.bar, t=t: eng.wait_ge(s.h, t))
            self.lastwait[e] = {}

    def finalize(self, blk):
        for e in self.ENGINES:
            def runner(eng, e=e):
                for f in self.prog[e]:
                    f(eng)
            getattr(blk, e)(runner)


class Sem:
    def __init__(self, h, name):
        self.h = h
        self.val = 0
        self.name = name


class Arena:
    def __init__(self, tile_f32, total):
        self.t = tile_f32
        self.total = total
        self.off = 0
        self.marks = []
        self.peak = 0

    def take(self, n_f32, dtype=F32, align=1):
        self.off = (self.off + align - 1) // align * align
        o = self.off
        self.off += int(n_f32)
        self.peak = max(self.peak, self.off)
        assert self.off <= self.total, f"arena overflow {self.off} > {self.total}"
        sl = self.t[:, o:o + int(n_f32)]
        return sl if dtype == F32 else sl.bitcast(dtype)

    def push(self):
        self.marks.append(self.off)

    def pop(self):
        self.off = self.marks.pop()


def build_program(cfg):
    S, D, DH, NR, NBK, G = (cfg[k] for k in ("S", "D", "DH", "NR", "NBK", "G"))
    NC, TOK, NB = cfg["NC"], cfg["TOK"], cfg["NB"]
    W = 2 * NBK
    HD = 4 * D
    NCH = S // 128
    NBLK = NR * S // 128
    dev = cfg["dev_taps"]

    nc = bass.Bass("TRN2", target_bir_lowering=False)
    dt = nc.dram_tensor

    x_in = [dt(f"x{b}", [S, D], F32, kind="ExternalInput").ap() for b in range(NB)]
    xf_in = [dt(f"xf{b}", [TOK, D], F32, kind="ExternalInput").ap() for b in range(NB)]
    wqkT = dt("wqkT", [D, 128], F32, kind="ExternalInput").ap()
    wvT = dt("wvT", [D, 128], F32, kind="ExternalInput").ap()
    woutT_d = dt("woutT", [128, D], F32, kind="ExternalInput").ap()
    rot2_d = dt("rot2", [128, NR * NBK], F32, kind="ExternalInput").ap()
    w1T_d = dt("w1T", [D, HD], F32, kind="ExternalInput").ap()
    w2T_d = dt("w2T", [D, HD], F32, kind="ExternalInput").ap()
    h1T_d = dt("h1T", [HD, HD], F32, kind="ExternalInput").ap()
    h2T_d = dt("h2T", [HD, HD], F32, kind="ExternalInput").ap()
    w3T_d = dt("w3T", [HD, D], F32, kind="ExternalInput").ap()
    NSM = 4 * (HD // 128) + D // 128
    smat_d = dt("smat", [128, NSM], F32, kind="ExternalInput").ap()
    iotaK_d = dt("iotaK", [128, NBK], F32, kind="ExternalInput").ap()
    iotaT_d = dt("iotaT", [128, 2 * W], F16, kind="ExternalInput").ap()
    ltri_d = dt("ltri", [128, 128], BF16, kind="ExternalInput").ap()
    stair_d = dt("stair", [128, 256], BF16, kind="ExternalInput").ap()
    onesc_d = dt("onesc", [128, 1], BF16, kind="ExternalInput").ap()
    ones1_d = dt("ones1", [1, 128], F32, kind="ExternalInput").ap()
    identb_d = dt("identb", [128, 128], BF16, kind="ExternalInput").ap()
    identh_d = dt("identh", [128, 128], F16, kind="ExternalInput").ap()
    identhh_d = dt("identhh", [128, 64], F16, kind="ExternalInput").ap()
    identb2_d = dt("identb2", [128, 64], BF16, kind="ExternalInput").ap()
    zero4_d = dt("zero4", [128, 8], F32, kind="ExternalInput").ap()

    y_out = [dt(f"y{b}", [TOK, D], F32, kind="ExternalOutput").ap() for b in range(NB)]

    sorted_d = dt("sorted_d", [NR * S, ROWW], F16, kind="Internal").ap()
    osort_d = dt("osort_d", [NR * S, OROW], F16, kind="Internal").ap()
    a_dram = dt("a_dram", [G, W], F32, kind="Internal").ap()
    heads_dr = dt("heads_d", [S, 128], BF16, kind="Internal").ap()
    phase = cfg.get("phase", "fused")
    ho_out = []
    htok_in = []
    woutF_d = None
    if phase == "A":
        ho_out = [dt(f"ho{b}", [S, 128], BF16, kind="ExternalOutput").ap()
                  for b in range(NB)]
    if phase == "F":
        htok_in = [dt(f"htok{b}", [TOK, D], BF16, kind="ExternalInput").ap()
                   for b in range(NB)]
        woutF_d = dt("woutF", [D, D], F32, kind="ExternalInput").ap()
    x1_dr = dt("x1_d", [TOK, D], BF16, kind="Internal").ap()
    part_dr = dt("part_d", [S, D], F32, kind="Internal").ap()
    rs_dr = dt("rs_d", [TOK, D], F32, kind="Internal").ap()
    ag_dr = dt("ag_d", [NC * TOK, 128], BF16, kind="Internal").ap()

    taps = {}
    if dev:
        taps["qkT"] = dt("tap_qkT", [128, S], BF16, kind="ExternalOutput").ap()
        taps["vT"] = dt("tap_vT", [128, S], BF16, kind="ExternalOutput").ap()
        taps["bkt"] = dt("tap_bkt", [128, 2 * NR * G], F32, kind="ExternalOutput").ap()
        taps["dest"] = dt("tap_dest", [128, 2 * NR * G], F32, kind="ExternalOutput").ap()
        taps["srt"] = dt("tap_srt", [NR * S, ROWW], F16, kind="ExternalOutput").ap()
        taps["osrt"] = dt("tap_osrt", [NR * S, OROW], F16, kind="ExternalOutput").ap()
        taps["heads"] = dt("tap_heads", [S, 128], BF16, kind="ExternalOutput").ap()
        taps["x1"] = dt("tap_x1", [TOK, D], F32, kind="ExternalOutput").ap()

    ctx = []
    def enter(h):
        t = h.__enter__()
        ctx.append(h)
        return t

    AR_TOT = 51200
    arena = enter(nc.sbuf_tensor("arena", [128, AR_TOT], F32))
    ar = Arena(arena, AR_TOT)

    c_iotaK = ar.take(NBK)
    c_iotaT = ar.take(W, F16)
    c_ltri = ar.take(64, BF16)
    c_stair = ar.take(128, BF16)
    c_onesc = ar.take(4, BF16)[:, 0:1]
    c_ones1 = ar.take(128)[0:1, :]
    c_identb = ar.take(64, BF16)
    c_identh = ar.take(64, F16)
    c_identhh = ar.take(32, F16)
    c_identb2 = ar.take(32, BF16)
    c_zero4 = ar.take(8)
    c_rot2 = ar.take(NR * NBK // 2, BF16)
    c_wqkT = ar.take(D // 2, BF16)
    c_wvT = ar.take(D // 2, BF16)
    c_smat = ar.take(NSM)
    CONST_END = ar.off

    p_a = enter(nc.psum_tensor([128, 1024], F32))
    p_b = enter(nc.psum_tensor([128, 1024], F32))
    p_c = enter(nc.psum_tensor([128, 512], F32))
    p_d = enter(nc.psum_tensor([128, 512], F32))
    p_e = enter(nc.psum_tensor([128, 512], F32))
    p_t = enter(nc.psum_tensor([128, 1024], F16))

    ek = EK(nc)
    for h in ek.ctx:
        ctx.append(h)

    TENS, VEC, SCA, SYN, GPS = "tensor", "vector", "scalar", "sync", "gpsimd"

    t0 = []
    for dst, src in [(c_iotaK, iotaK_d), (c_iotaT, iotaT_d), (c_ltri, ltri_d),
                     (c_stair, stair_d), (c_onesc, onesc_d), (c_ones1, ones1_d),
                     (c_identb, identb_d), (c_identh, identh_d),
                     (c_identhh, identhh_d), (c_identb2, identb2_d), (c_zero4, zero4_d), (c_smat, smat_d)]:
        t0.append(ek.dma(SYN, dst, src[:]))
    t0.append(ek.dma(GPS, c_rot2, rot2_d[:]))
    t0.append(ek.dma(GPS, c_wqkT.rearrange("p (k n) -> p k n", n=128),
                     wqkT.rearrange("(k p) n -> p k n", p=128)))
    t0.append(ek.dma(GPS, c_wvT.rearrange("p (k n) -> p k n", n=128),
                     wvT.rearrange("(k p) n -> p k n", p=128)))
    ek.barrier()

    def emit_A(b):
        if phase == "F":
            return
        ar.push()
        a_qkT = ar.take(S // 2, BF16)
        a_vT = ar.take(S // 2, BF16)
        a_rv = ar.take(2 * NR * NBK)
        a_sm = ar.take(8 * NR)
        a_smu = ar.take(NR, U8)[:, 0:4]
        a_eq = ar.take(NR * NBK // 2, F16)
        a_eq2 = ar.take(NR * NBK // 2, F16)
        a_bkt = ar.take(NR * G)
        a_nsm = ar.take(8)
        a_o = ar.take(2 * 2 * W // 2, BF16)
        a_tot = ar.take(W)[0:1, :]
        a_tsh = ar.take(W + 4)[0:1, :]
        a_ones_row = ar.take(W)[0:1, :]
        a_start = ar.take(W)[0:1, :]
        a_amat = ar.take(W)
        a_ttmp = ar.take(W // 2, BF16)
        a_destF = ar.take(NR * G)
        a_destU = ar.take(NR * G, U32)
        R1 = ar.take(max(128 * (S // 128) * ROWW // 2 // 128, (S + 128) // 2))
        a_rowsF = R1.bitcast(F16)
        a_khT = R1.bitcast(F16)
        R2 = ar.take(8192)
        a_xrow = R2[:, 0:512].bitcast(BF16)
        a_xrow = R2[:, 0:1024].bitcast(BF16)
        a_xT = R2[:, 1024:3072].bitcast(BF16)
        a_aflat = R2[0:1, 0:8192]
        sub = Arena(R2, 8192)
        sub.off = 0
        a_stash = sub.take(NR * 64 // 2 + NR, F16)
        a_panel = sub.take(16 * ROWW // 2, F16)
        a_vwA = sub.take(2 * 8 * 68 // 2, F16)
        a_vwB = sub.take(2 * 8 * 68 // 2, F16)
        a_ptile = sub.take(96, F16)
        a_pt2 = sub.take(2 * 64, F16)
        a_asm = sub.take(32)
        a_orow = sub.take(8 * OROW // 2, F16)
        a_gath = sub.take(2 * NR * OROW // 2, F16)
        a_csm = sub.take(16)
        a_hrow = sub.take(4 * 128 // 2, BF16)

        UPTO = cfg.get("upto", "full")
        qkT3 = a_qkT.rearrange("p (n t) -> p n t", t=512)
        vT3 = a_vT.rearrange("p (n t) -> p n t", t=512)
        xT3 = a_xT.rearrange("p (k t) -> p k t", t=512)
        xrow2 = a_xrow.rearrange("p (n d) -> p n d", d=D)

        tok_qkT = [None] * (S // 512)
        tok_vT = [None] * (S // 512)
        xld = [None, None]
        xuse = [None, None]
        pc_free = None
        pd_free = None
        for tt in range(S // 512):
            trans_toks = []
            for q in range(4):
                buf = (tt * 4 + q) % 2
                xld[buf] = ek.dma(GPS, xrow2[:, buf, :],
                                  x_in[b][(tt * 4 + q) * 128:(tt * 4 + q) * 128 + 128, :],
                                  ch=f"x{buf}",
                                  deps=[xuse[buf]] if xuse[buf] else [])
                for k in range(8):
                    tk = ek.op(TENS, lambda eng, buf=buf, k=k: eng.transpose(
                        p_t.bitcast(BF16)[:, 0:128],
                        xrow2[:, buf, k * 128:(k + 1) * 128], c_identb),
                        deps=[xld[buf], trans_toks[-1] if trans_toks else None])
                    ev = ek.op(VEC, lambda eng, k=k, q=q: eng.tensor_copy(
                        out=xT3[:, k, q * 128:(q + 1) * 128],
                        in_=p_t.bitcast(BF16)[:, 0:128]), deps=[tk])
                    trans_toks.append(ev)
                xuse[buf] = trans_toks[-1]
            wq = c_wqkT.rearrange("p (k n) -> p k n", n=128)
            wv = c_wvT.rearrange("p (k n) -> p k n", n=128)
            for which, wmat, psum in ((0, wq, p_c), (1, wv, p_d)):
                mmt = None
                deps = list(trans_toks[-1:]) + [pc_free if which == 0 else pd_free]
                for k in range(8):
                    mmt = ek.op(TENS, lambda eng, wmat=wmat, k=k, psum=psum: eng.matmul(
                        psum[:], wmat[:, k, :], xT3[:, k, :],
                        start=(k == 0), stop=(k == 7)),
                        deps=deps if k == 0 else ())
                dstT = qkT3 if which == 0 else vT3
                ev = ek.op(VEC, lambda eng, dstT=dstT, tt=tt, psum=psum: eng.tensor_copy(
                    out=dstT[:, tt, :], in_=psum[:]), deps=[mmt])
                if which == 0:
                    tok_qkT[tt] = ev
                    pc_free = ev
                else:
                    tok_vT[tt] = ev
                    pd_free = ev
        tok_projdone = [tok_qkT[-1], tok_vT[-1]]

        if dev:
            ek.dma(SYN, taps["qkT"][:], a_qkT, ch="tap", deps=tok_projdone)
            ek.dma(SYN, taps["vT"][:], a_vT, ch="tap", deps=tok_projdone)
        ek.barrier()

        rows3 = a_rowsF[:, 0:NCH * ROWW].rearrange("p (g w) -> p g w", w=ROWW)

        for h2 in range(2):
            hb = h2 * 64
            prv = [p_a.rearrange("p (n w) -> p n w", w=NR * NBK)[:, 0, :],
                   p_b.rearrange("p (n w) -> p n w", w=NR * NBK)[:, 0, :]]
            rv2 = a_rv.rearrange("p (n w) -> p n w", w=NR * NBK)
            sm8 = a_sm.rearrange("p (k r) -> p k r", r=NR)
            bkt3 = a_bkt.rearrange("p (r g) -> p r g", g=G)
            rvfree = [None, None]
            ptfree = None
            rows_tok = [None] * NCH
            for g in range(NCH):
                buf = g % 2
                nmm = (NR * NBK + 511) // 512
                mmt = None
                for mi in range(nmm):
                    n0, n1 = mi * 512, min(NR * NBK, (mi + 1) * 512)
                    mmt = ek.op(TENS, lambda eng, g=g, buf=buf, n0=n0, n1=n1, hb=hb: eng.matmul(
                        prv[buf][:, n0:n1],
                        a_qkT[hb:hb + 64, g * 128:(g + 1) * 128],
                        c_rot2[hb:hb + 64, n0:n1], start=True, stop=True),
                        deps=[tok_projdone[0], rvfree[buf]] if mi == 0 else ())
                tq = ek.op(TENS, lambda eng, g=g, hb=hb: eng.transpose(
                    p_t.bitcast(BF16)[:, 0:64],
                    a_qkT[hb:hb + 64, g * 128:(g + 1) * 128], c_identb2[hb:hb + 64, :]),
                    deps=[ptfree, tok_projdone[1]])
                tv = ek.op(TENS, lambda eng, g=g, hb=hb: eng.transpose(
                    p_t.bitcast(BF16)[:, 64:128],
                    a_vT[hb:hb + 64, g * 128:(g + 1) * 128], c_identb2[hb:hb + 64, :]),
                    deps=())
                tsq = ek.op(SCA, lambda eng: eng.activation(
                    a_ttmp.bitcast(F16)[:, 0:64], p_t.bitcast(BF16)[:, 0:64],
                    AF.Square, accum_out=a_nsm[:, 0:1]), deps=[tq])
                tsm = ek.op(SCA, lambda eng: eng.activation(
                    a_nsm[:, 1:2], a_nsm[:, 0:1], AF.Sqrt,
                    scale=1.0 / 64.0, bias=c_zero4[:, 4:5]), deps=[tsq])
                tc = ek.op(SCA, lambda eng: eng.activation(
                    a_nsm[:, 2:3], a_nsm[:, 1:2], AF.Copy, scale=8.0), deps=[tsm])
                trc = ek.op(VEC, lambda eng: eng.reciprocal(
                    a_nsm[:, 3:4], a_nsm[:, 1:2]), deps=[tsm])
                tsc = ek.op(VEC, lambda eng: eng.tensor_scalar(
                    a_nsm[:, 4:5], a_nsm[:, 3:4], 0.125, None, op0=ALU.mult),
                    deps=[trc])
                tk1 = ek.op(VEC, lambda eng, g=g: eng.tensor_scalar(
                    rows3[:, g, 0:64], p_t.bitcast(BF16)[:, 0:64],
                    a_nsm[:, 4:5], None, op0=ALU.mult), deps=[tsc, tq])
                tk2 = ek.op(VEC, lambda eng, g=g: eng.tensor_copy(
                    out=rows3[:, g, 64:65], in_=a_nsm[:, 2:3]), deps=[tc])
                tk3 = ek.op(VEC, lambda eng, g=g: eng.tensor_copy(
                    out=rows3[:, g, 68:132], in_=p_t.bitcast(BF16)[:, 64:128]),
                    deps=[tv])
                tk4 = ek.op(VEC, lambda eng, g=g: eng.tensor_copy(
                    out=rows3[:, g, 65:68], in_=c_zero4[:, 0:3]), deps=())
                ptfree = tk3
                rows_tok[g] = (tk1, tk2, tk3, tk4)
                rvp3 = prv[buf].rearrange("p (r k) -> p r k", k=NBK)
                tmx = ek.op(VEC, lambda eng, rvp3=rvp3: eng.tensor_reduce(
                    sm8[:, 0, :], rvp3, axis=AX.X, op=ALU.max), deps=[mmt])
                tmn = ek.op(VEC, lambda eng, rvp3=rvp3: eng.tensor_reduce(
                    sm8[:, 1, :], rvp3, axis=AX.X, op=ALU.min), deps=[mmt])
                tev = ek.op(SCA, lambda eng, buf=buf: eng.activation(
                    rv2[:, buf, :], prv[buf], AF.Copy), deps=[mmt, tc])
                tsum = ek.op(VEC, lambda eng: eng.tensor_tensor(
                    sm8[:, 2, :], sm8[:, 0, :], sm8[:, 1, :], op=ALU.add),
                    deps=[tmx, tmn])
                tps = ek.op(VEC, lambda eng: eng.tensor_tensor(
                    a_smu[:, 0:NR], sm8[:, 2, :], c_zero4[:, 0:NR], op=ALU.is_ge),
                    deps=[tsum])
                tsel = ek.op(VEC, lambda eng: eng.select(
                    sm8[:, 3, :], a_smu[:, 0:NR], sm8[:, 0, :], sm8[:, 1, :],
                    add_drain=True), deps=[tps], n=2)
                toff = ek.op(VEC, lambda eng: eng.tensor_scalar(
                    sm8[:, 4, :], a_smu[:, 0:NR], -float(NBK), float(NBK),
                    op0=ALU.mult, op1=ALU.add), deps=[tps])
                eq3 = a_eq.rearrange("p (r k) -> p r k", k=NBK)
                eq23 = a_eq2.rearrange("p (r k) -> p r k", k=NBK)
                selb = sm8[:, 3, :].unsqueeze(2).broadcast_to((128, NR, NBK))
                teq = ek.op(VEC, lambda eng, buf=buf, selb=selb, eq3=eq3: eng.tensor_tensor(
                    eq3, rv2[:, buf, :].rearrange("p (r k) -> p r k", k=NBK), selb,
                    op=ALU.is_equal), deps=[tsel, tev])
                iob = c_iotaK.unsqueeze(1).broadcast_to((128, NR, NBK))
                tdot = ek.op(VEC, lambda eng, eq3=eq3, eq23=eq23, iob=iob: eng.tensor_tensor(
                    eq23, eq3, iob, op=ALU.mult), deps=[teq])
                rvfree[buf] = tdot
                tred = ek.op(VEC, lambda eng, eq23=eq23, g=g: eng.tensor_reduce(
                    bkt3[:, :, g], eq23, axis=AX.X, op=ALU.add), deps=[tdot])
                tbk = ek.op(VEC, lambda eng, g=g: eng.tensor_tensor(
                    bkt3[:, :, g], bkt3[:, :, g], sm8[:, 4, :], op=ALU.add),
                    deps=[tred, toff])
                bkt_done = tbk

            if dev:
                ek.dma(SYN, taps["bkt"][:, h2 * NR * G:(h2 + 1) * NR * G],
                       a_bkt, ch="tap", deps=[bkt_done])

            if UPTO == "A2":
                ek.barrier()
                continue
            o2 = a_o.rearrange("p (n w) -> p n w", w=W)
            destF3 = a_destF.rearrange("p (r g) -> p r g", g=G)
            destU3 = a_destU.rearrange("p (r g) -> p r g", g=G)
            scat_toks = []
            prev_ab = []
            prev_afl_use = []
            for r in range(NR):
                ofree = [None, None]
                pmm = None
                tmm = None
                for g in range(G):
                    ob = g % 2
                    to = ek.op(VEC, lambda eng, r=r, g=g, ob=ob: eng.tensor_scalar(
                        o2[:, ob, :], c_iotaT[:, 0:W], bkt3[:, r, g:g + 1], None,
                        op0=ALU.is_equal), deps=[bkt_done, ofree[ob]])
                    pmm = ek.op(TENS, lambda eng, g=g, ob=ob: eng.matmul(
                        p_c[:, 0:W], c_stair[:, 127 - g:255 - g], o2[:, ob, :],
                        start=(g == 0), stop=(g == G - 1)), deps=[to])
                    tmm = ek.op(TENS, lambda eng, g=g, ob=ob: eng.matmul(
                        p_d[0:1, 0:W], c_onesc[:], o2[:, ob, :],
                        start=(g == 0), stop=(g == G - 1)), deps=())
                    ofree[ob] = tmm
                tte = ek.op(VEC, lambda eng: eng.tensor_copy(
                    out=a_tot[:], in_=p_d[0:1, 0:W]), deps=[tmm])
                tz = ek.op(VEC, lambda eng: eng.tensor_scalar(
                    a_tsh[0:1, 0:1], a_tot[0:1, 0:1], 0.0, None, op0=ALU.mult),
                    deps=[tte])
                tsh2 = ek.op(VEC, lambda eng: eng.tensor_copy(
                    out=a_tsh[0:1, 1:W], in_=a_tot[0:1, 0:W - 1]), deps=[tte])
                tone = ek.op(VEC, lambda eng: eng.tensor_scalar(
                    a_ones_row[:], a_tot[:], 0.0, 1.0, op0=ALU.mult, op1=ALU.add),
                    deps=[tte])
                tsc2 = ek.op(VEC, lambda eng: eng.tensor_tensor_scan(
                    a_start[:], a_ones_row[:], a_tsh[0:1, 0:W], 0.0,
                    op0=ALU.mult, op1=ALU.add), deps=[tz, tsh2, tone])
                fmm = ek.op(TENS, lambda eng: eng.matmul(
                    p_c[:, 0:W], c_ones1[:], a_start[:], start=False, stop=True,
                    skip_group_check=True), deps=[tsc2, pmm])
                tam = ek.op(SCA, lambda eng: eng.activation(
                    a_amat[:, 0:W], p_c[:, 0:W], AF.Copy),
                    deps=[fmm] + prev_ab)
                tad = ek.dma(SYN, a_dram[:, :], a_amat[0:G, 0:W], ch="ab",
                             deps=[tam] + prev_afl_use)
                NAB = 8
                afl = [None, None]
                afl_use = [None, None]
                rfree = [None, None]
                exq = None
                for g in range(G):
                    ab = (g // NAB) % 2
                    if g % NAB == 0:
                        afl[ab] = ek.dma(SYN, a_aflat[0:1, ab * NAB * W:(ab + 1) * NAB * W],
                                         a_dram.rearrange("g w -> (g w)")
                                         [g * W:(g + NAB) * W].unsqueeze(0),
                                         ch=f"af{ab}", deps=[tad, afl_use[ab]])
                    ob = g % 2
                    to = ek.op(VEC, lambda eng, r=r, g=g, ob=ob: eng.tensor_scalar(
                        o2[:, ob, :], c_iotaT[:, 0:W], bkt3[:, r, g:g + 1], None,
                        op0=ALU.is_equal), deps=[ofree[ob]])
                    pr = p_e if (g % 2 == 0) else p_c
                    rmm = ek.op(TENS, lambda eng, g=g, ob=ob, pr=pr: eng.matmul(
                        pr[:, 0:W], c_ltri[:, 0:128], o2[:, ob, :],
                        start=True, stop=False, skip_group_check=True),
                        deps=[to, rfree[g % 2], tam if g == 0 else None])
                    fm2 = ek.op(TENS, lambda eng, g=g, ab=ab, pr=pr: eng.matmul(
                        pr[:, 0:W], c_ones1[:],
                        a_aflat[0:1, (ab * NAB + (g % NAB)) * W:(ab * NAB + g % NAB + 1) * W],
                        start=False, stop=True, skip_group_check=True),
                        deps=[afl[ab]])
                    afl_use[ab] = fm2
                    tex = ek.op(VEC, lambda eng, r=r, g=g, ob=ob, pr=pr: eng.tensor_tensor_reduce(
                        a_ttmp[:, 0:W], pr[:, 0:W], o2[:, ob, :],
                        1.0, 0.0, op0=ALU.mult, op1=ALU.add,
                        accum_out=destF3[:, r, g:g + 1]), deps=[fm2])
                    ofree[ob] = tex
                    rfree[g % 2] = tex
                    exq = tex
                prev_ab = [tad]
                prev_afl_use = [t for t in afl_use if t]
                tcl = ek.op(VEC, lambda eng, r=r: eng.tensor_scalar(
                    destF3[:, r, :], destF3[:, r, :], float(S - 1), 0.0,
                    op0=ALU.min, op1=ALU.max), deps=[exq])
                tcu = ek.op(VEC, lambda eng, r=r: eng.tensor_copy(
                    out=destU3[:, r, :], in_=destF3[:, r, :]), deps=[tcl])
                for g in range(G):
                    stt = ek.idma(sorted_d, rows3[:, g, :], ch="scat",
                                  deps=[tcu] + list(rows_tok[g]),
                                  out_offset=IOA(ap=destU3[:, r, g:g + 1], axis=0),
                                  element_offset=r * S * ROWW)
                    scat_toks.append(stt)
            if dev:
                ek.dma(SYN, taps["dest"][:, h2 * NR * G:(h2 + 1) * NR * G],
                       a_destF, ch="tap", deps=[tcu])
            ek.barrier()
            if dev:
                ek.dma(SYN, taps["srt"][:, :], sorted_d[:, :], ch="tap")
                ek.barrier()

            if UPTO == "A3":
                continue
            pan3 = a_panel.rearrange("p (n w) -> p n w", w=8 * ROWW)
            orow3 = a_orow.rearrange("p (n w) -> p n w", w=OROW)
            srt4 = sorted_d.rearrange("(a blk p) w -> a p blk w", p=128, blk=8)
            pan4 = a_panel.rearrange("p (n blk w) -> p n blk w", blk=8, w=ROWW)
            tst = None
            tt1 = None
            for r in range(NR):
                tld = ek.dma(SYN, a_panel[0:64, 0:ROWW],
                             sorted_d[r * S + S - 64:r * S + S, :], ch="pan0",
                             deps=[tt1])
                tt1 = ek.op(TENS, lambda eng, r=r: eng.transpose(
                    p_t[0:65, 0:64], a_panel[0:64, 0:65], c_identh[0:64, 0:64]),
                    deps=[tld, tst])
                tst = ek.op(SCA, lambda eng, r=r: eng.activation(
                    a_stash[0:65, r * 64:(r + 1) * 64], p_t[0:65, 0:64], AF.Copy),
                    deps=[tt1])

            if UPTO == "A4a":
                ek.barrier()
                continue
            exq2 = None
            prev_vAB = [tst]
            pfree = [None, None]
            pl_tok = [None, None]
            tbv_use = None
            tbv = None
            st_prev = None
            for r in range(NR):
                kev = tst
                for pb in range(NCH // 8):
                    pbuf = pb % 2
                    if cfg.get("smallpan"):
                        for j8 in range(8):
                            tpl = ek.dma(SYN, pan4[:, pbuf, j8], srt4[r * (NCH // 8) + pb, :, j8],
                                         ch=f"pan{pbuf}", deps=[kev, pfree[pbuf]] if j8 == 0 else [])
                    else:
                        tpl = ek.dma(SYN, pan4[:, pbuf], srt4[r * (NCH // 8) + pb],
                                     ch=f"pan{pbuf}", deps=[kev, pfree[pbuf]])
                    for j in range(8):
                        m = pb * 8 + j
                        tt1 = ek.op(TENS, lambda eng, pbuf=pbuf, j=j: eng.transpose(
                            p_t[0:65, 0:128], pan3[:, pbuf, j * ROWW:j * ROWW + 65],
                            c_identh[:, :]), deps=[tpl, kev])
                        kev = ek.op(SCA, lambda eng, m=m: eng.activation(
                            a_khT[0:65, m * 128:(m + 1) * 128], p_t[0:65, 0:128],
                            AF.Copy), deps=[tt1])
                    pfree[pbuf] = kev
                vwA4 = a_vwA.rearrange("p (n blk w) -> p n blk w", blk=8, w=68)
                vwB4 = a_vwB.rearrange("p (n blk w) -> p n blk w", blk=8, w=68)
                if UPTO == "A4b":
                    continue
                for pb in range(NCH // 8):
                    pbuf = pb % 2
                    tB = ek.dma(SYN, vwB4[:, pbuf],
                                sorted_d[r * S + pb * 1024:r * S + (pb + 1) * 1024, 64:132]
                                .rearrange("(blk p) w -> p blk w", p=128),
                                ch=f"vb{pbuf}", deps=[pfree[pbuf], kev])
                    if pb == 0:
                        tA0 = ek.dma(SYN, vwA4[0:64, pbuf, 0],
                                     sorted_d[((r - 1) % NR) * S + S - 64:
                                              ((r - 1) % NR) * S + S, 64:132],
                                     ch=f"va{pbuf}", deps=[pfree[pbuf], kev])
                        tA1 = ek.dma(SYN, vwA4[64:128, pbuf, 0],
                                     sorted_d[r * S:r * S + 64, 64:132],
                                     ch=f"va{pbuf}", deps=[])
                        tA = ek.dma(SYN, vwA4[:, pbuf, 1:8].rearrange("p blk w -> p blk w"),
                                    sorted_d[r * S + 64:r * S + 64 + 896, 64:132]
                                    .rearrange("(blk p) w -> p blk w", p=128),
                                    ch=f"va{pbuf}", deps=[])
                    else:
                        tA = ek.dma(SYN, vwA4[:, pbuf],
                                    sorted_d[r * S + pb * 1024 - 64:
                                             r * S + (pb + 1) * 1024 - 64, 64:132]
                                    .rearrange("(blk p) w -> p blk w", p=128),
                                    ch=f"va{pbuf}", deps=[pfree[pbuf], kev])
                    pl_tok[pbuf] = tA if pb > 0 else tA
                    plB = tB
                    for j in range(8):
                        m = pb * 8 + j
                        d1 = ek.op(TENS, lambda eng, m=m: eng.matmul(
                            p_c[:, 64:192], a_khT[0:64, m * 128:m * 128 + 128],
                            a_khT[0:64, m * 128:m * 128 + 128],
                            start=True, stop=True), deps=[kev, exq2])
                        if m > 0:
                            d2 = ek.op(TENS, lambda eng, m=m: eng.matmul(
                                p_c[:, 0:64], a_khT[0:64, m * 128:m * 128 + 128],
                                a_khT[0:64, m * 128 - 64:m * 128],
                                start=True, stop=True), deps=())
                        else:
                            d2 = ek.op(TENS, lambda eng, r=r: eng.matmul(
                                p_c[:, 0:64], a_khT[0:64, 0:128],
                                a_stash[0:64, ((r - 1) % NR) * 64:((r - 1) % NR) * 64 + 64],
                                start=True, stop=True), deps=[tst])
                        tc1 = ek.op(VEC, lambda eng, pbuf=pbuf, j=j: eng.tensor_copy(
                            out=a_asm[:, 0:1], in_=vwB4[:, pbuf, j, 0:1]),
                            deps=[plB, exq2])
                        mA = ek.op(VEC, lambda eng: eng.tensor_reduce(
                            a_asm[0:64, 1:2], p_c[0:64, 0:128], axis=AX.X,
                            op=ALU.max), deps=[d1, d2])
                        mB = ek.op(VEC, lambda eng: eng.tensor_reduce(
                            a_asm[64:128, 1:2], p_c[64:128, 64:192], axis=AX.X,
                            op=ALU.max), deps=[d1])
                        tb = ek.op(VEC, lambda eng: eng.tensor_tensor(
                            a_asm[:, 2:3], a_asm[:, 0:1], a_asm[:, 1:2], op=ALU.mult),
                            deps=[mA, mB, tc1])
                        tnb = ek.op(VEC, lambda eng: eng.tensor_scalar(
                            a_asm[:, 3:4], a_asm[:, 2:3], -1.0, None, op0=ALU.mult),
                            deps=[tb])
                        eA = ek.op(SCA, lambda eng: eng.activation(
                            a_ptile[0:64, 0:128], p_c[0:64, 0:128],
                            AF.Exp, bias=a_asm[0:64, 3:4], scale=a_asm[0:64, 0:1],
                            accum_out=a_asm[0:64, 4:5]), deps=[tnb, prev_vAB[-1]])
                        eB = ek.op(SCA, lambda eng: eng.activation(
                            a_ptile[64:128, 64:192], p_c[64:128, 64:192],
                            AF.Exp, bias=a_asm[64:128, 3:4], scale=a_asm[64:128, 0:1],
                            accum_out=a_asm[64:128, 4:5]), deps=[tnb])
                        if UPTO == "A4c":
                            exq2 = eB
                            prev_vAB = [eA, eB]
                            pfree[pbuf] = eB
                            if j == 0:
                                pfree[1 - pbuf] = eB
                            continue
                        tln = ek.op(SCA, lambda eng: eng.activation(
                            a_asm[:, 5:6], a_asm[:, 4:5], AF.Ln), deps=[eA, eB])
                        tlg = ek.op(VEC, lambda eng: eng.tensor_tensor(
                            a_asm[:, 6:7], a_asm[:, 5:6], a_asm[:, 2:3], op=ALU.add),
                            deps=[tln])
                        if UPTO == "A4e":
                            exq2 = tlg
                            prev_vAB = [eA, eB]
                            pfree[pbuf] = tlg
                            if j == 0:
                                pfree[1 - pbuf] = tlg
                            continue
                        pA1 = ek.op(TENS, lambda eng: eng.transpose(
                            p_t[:, 0:64], a_ptile[0:64, 0:128],
                            c_identh[0:64, 0:64]), deps=[eA])
                        pA2 = pA1
                        if UPTO in ("A4f", "A4g"):
                            pB = pA2
                        else:
                            pB = ek.op(TENS, lambda eng: eng.transpose(
                                p_t[:, 64:128], a_ptile[64:128, 64:192],
                                c_identhh[64:128, :]), deps=[eB])
                        vA = ek.op(SCA, lambda eng: eng.activation(
                            a_pt2[:, 0:64], p_t[:, 0:64], AF.Copy), deps=[pA1, pA2])
                        if UPTO in ("A4f", "A4g"):
                            vB = vA
                        else:
                            vB = ek.op(SCA, lambda eng: eng.activation(
                                a_pt2[:, 64:128], p_t[:, 64:128], AF.Copy), deps=[pB])
                        prev_vAB = [vA, vB]
                        if UPTO == "A4f":
                            exq2 = vA
                            pfree[pbuf] = vA
                            if j == 0:
                                pfree[1 - pbuf] = vA
                            continue
                        if UPTO == "A4d":
                            exq2 = vB
                            pfree[pbuf] = vB
                            if j == 0:
                                pfree[1 - pbuf] = vB
                            continue
                        a2 = ek.op(TENS, lambda eng, pbuf=pbuf, j=j: eng.matmul(
                            p_d[0:64, 0:64], a_pt2[:, 0:64], vwA4[:, pbuf, j, 4:68],
                            start=True, stop=True), deps=[vA, pl_tok[pbuf]])
                        a4 = ek.op(TENS, lambda eng, pbuf=pbuf, j=j: eng.matmul(
                            p_d[64:128, 0:64], a_pt2[:, 64:128], vwB4[:, pbuf, j, 4:68],
                            start=True, stop=True), deps=[vB, plB])
                        rc = ek.op(VEC, lambda eng: eng.reciprocal(
                            a_asm[:, 7:8], a_asm[:, 4:5]), deps=[eA, eB])
                        ob2 = m % 8
                        no = ek.op(VEC, lambda eng, ob2=ob2: eng.tensor_scalar(
                            orow3[:, ob2, 0:64], p_d[:, 0:64], a_asm[:, 7:8], None,
                            op0=ALU.mult),
                            deps=[rc, a2, a4] + ([st_prev] if ob2 == 0 else []))
                        exq2 = no
                        lg2 = ek.op(VEC, lambda eng, ob2=ob2: eng.tensor_copy(
                            out=orow3[:, ob2, 64:65], in_=a_asm[:, 6:7]), deps=[tlg])
                        zz = ek.op(VEC, lambda eng, ob2=ob2: eng.tensor_copy(
                            out=orow3[:, ob2, 65:68], in_=c_zero4.bitcast(F16)[:, 0:3]),
                            deps=())
                        pfree[pbuf] = no
                        if j == 0:
                            pfree[1 - pbuf] = no
                        if j == 7:
                            if cfg.get("smallpan"):
                                for j8 in range(8):
                                    st_prev = st = ek.dma(SCA, osort_d.rearrange(
                                        "(a blk p) w -> a p blk w", p=128, blk=8)
                                        [r * (NCH // 8) + pb, :, j8],
                                        a_orow.rearrange("p (blk w) -> p blk w", w=OROW)[:, j8],
                                        ch="ost", deps=[no, lg2, zz] if j8 == 0 else [])
                            else:
                                st_prev = st = ek.dma(SCA, osort_d.rearrange(
                                    "(a blk p) w -> a p blk w", p=128, blk=8)
                                    [r * (NCH // 8) + pb],
                                    a_orow.rearrange("p (blk w) -> p blk w", w=OROW),
                                    ch="ost", deps=[no, lg2, zz])
            ek.barrier()
            if dev:
                ek.dma(SYN, taps["osrt"][:, :], osort_d[:, :], ch="tap")
                ek.barrier()

            if UPTO == "A4":
                ek.barrier()
                continue
            g3 = a_gath.rearrange("p (n r w) -> p n r w", r=NR, w=OROW)
            hb_prev = None
            gfree = [None, None]
            for g in range(G):
                gb = g % 2
                gts = []
                for r in range(NR):
                    gt = ek.idma(g3[:, gb, r, :], osort_d, ch=f"bw{gb}",
                                 deps=[gfree[gb]] if r == 0 else [],
                                 in_offset=IOA(ap=destU3[:, r, g:g + 1], axis=0),
                                 element_offset=r * S * OROW)
                    gts.append(gt)
                lcols = g3[:, gb, :, 64:65].squeeze(2)
                tlm = ek.op(VEC, lambda eng, lcols=lcols: eng.tensor_reduce(
                    a_csm[:, 4:5], lcols, axis=AX.X, op=ALU.max), deps=[gts[-1]])
                tng = ek.op(VEC, lambda eng: eng.tensor_scalar(
                    a_csm[:, 5:6], a_csm[:, 4:5], -1.0, None, op0=ALU.mult), deps=[tlm])
                tpw = ek.op(SCA, lambda eng, lcols=lcols: eng.activation(
                    a_csm[:, 0:4], lcols, AF.Exp, bias=a_csm[:, 5:6],
                    accum_out=a_csm[:, 6:7]), deps=[tng])
                trc2 = ek.op(VEC, lambda eng: eng.reciprocal(
                    a_csm[:, 7:8], a_csm[:, 6:7]), deps=[tpw])
                tw = ek.op(VEC, lambda eng: eng.tensor_scalar(
                    a_csm[:, 0:4], a_csm[:, 0:4], a_csm[:, 7:8], None,
                    op0=ALU.mult), deps=[trc2])
                acc = a_ptile.bitcast(F32)[:, 0:64]
                tacc = ek.op(VEC, lambda eng, gb=gb: eng.tensor_scalar(
                    acc, g3[:, gb, 0, 0:64], a_csm[:, 0:1], None, op0=ALU.mult),
                    deps=[tw])
                for r in range(1, NR):
                    tacc = ek.op(VEC, lambda eng, gb=gb, r=r: eng.scalar_tensor_tensor(
                        acc, g3[:, gb, r, 0:64], a_csm[:, r:r + 1], acc,
                        op0=ALU.mult, op1=ALU.add), deps=[tacc])
                th = ek.op(VEC, lambda eng: eng.tensor_copy(
                    out=a_hrow[:, 0:64], in_=acc), deps=[tacc, hb_prev])
                gfree[gb] = th
                hb_prev = ek.dma(SCA, heads_dr[g * 128:(g + 1) * 128,
                                               h2 * 64:(h2 + 1) * 64],
                                 a_hrow[:, 0:64], ch="hd", deps=[th])
            ek.barrier()
        if dev:
            ek.dma(SYN, taps["heads"][:, :], heads_dr[:, :], ch="tap")
        if phase == "A":
            ek.dma(SYN, ho_out[b][:, :], heads_dr[:, :], ch="ho")
        ek.barrier()
        ar.pop()
        return dict()

    def emit_F(b):
        if cfg.get("upto", "full") not in ("full", "F1", "F"):
            return
        if phase == "A":
            return
        ar.push()
        f_attnT = ar.take(2 * 128 // 2, BF16, align=8)
        f_woc = ar.take(D // 2, BF16) if phase != "F" else None
        f_wof = None
        f_x1T = ar.take(8 * min(1024, TOK) // 2, BF16, align=8)
        f_h = ar.take(32 * min(1024, TOK) // 2, BF16)
        f_h2 = ar.take(32 * min(1024, TOK) // 2, BF16)
        f_wp = ar.take(2 * 32 * 128 // 2, BF16)
        f_xf = ar.take(D)
        f_pp = ar.take(2 * D)
        f_r = ar.take(D)
        f_x1s = ar.take(D // 2, BF16)
        f_sm = ar.take(16)
        f_y = ar.take(D)
        f_z = f_y.bitcast(BF16)
        TSL = min(1024, TOK)
        NSL = TOK // TSL
        at4 = f_attnT.rearrange("p (n t) -> p n 1 t", n=2) if False else f_attnT.rearrange("p (n t) -> p n t", n=2)
        x1T3 = f_x1T.rearrange("p (n t) -> p n t", t=TSL)
        h3 = f_h.rearrange("p (n t) -> p n t", t=TSL)
        h23 = f_h2.rearrange("p (n t) -> p n t", t=TSL)
        wp3 = f_wp.rearrange("p (n k c) -> p n k c", n=2, c=128)
        yT3 = x1T3
        xf2 = f_xf.unsqueeze(1).transpose([0, 2, 1]) if False else f_xf.rearrange("p (n d) -> p n d", d=D, n=1)
        fpp2 = f_pp.rearrange("p (n d) -> p n d", d=D)
        x1s2 = f_x1s.rearrange("p (n d) -> p n d", d=D, n=1)

        if phase == "F":
            wof3 = f_h[:, 0:8 * D].rearrange("p (k c) -> p k c", c=D)
            two = ek.dma(GPS, wof3, woutF_d.rearrange("(k p) c -> p k c", p=128),
                         ch="wld")
        else:
            two = ek.dma(GPS, f_woc, woutT_d[:, :], ch="wld")
        atu = [None, None]
        ppu_prev = None
        st_pp = None
        for tt in range(0 if phase == "F" else S // 128):
            bb = tt % 2
            tat = ek.dma(SYN, at4[:, bb, :],
                         heads_dr[tt * 128:(tt + 1) * 128, :],
                         ch=f"at{bb}", deps=[atu[bb]], transpose=True)
            mmt = None
            for dh in range(2):
                mmt = ek.op(TENS, lambda eng, bb=bb, dh=dh: eng.matmul(
                    (p_a if dh == 0 else p_b)[:, 0:512],
                    at4[:, bb, :], f_woc[:, dh * 512:(dh + 1) * 512],
                    start=True, stop=True),
                    deps=[tat, two, ppu_prev] if dh == 0 else ())
            atu[bb] = mmt
            e1 = ek.op(VEC, lambda eng, bb=bb: eng.tensor_copy(
                out=fpp2[:, bb, 0:512], in_=p_a[:, 0:512]), deps=[mmt, st_pp])
            e2 = ek.op(VEC, lambda eng, bb=bb: eng.tensor_copy(
                out=fpp2[:, bb, 512:1024], in_=p_b[:, 0:512]), deps=[mmt])
            ppu_prev = e2
            st_pp = ek.dma(SCA, part_dr[tt * 128:(tt + 1) * 128, :],
                           fpp2[:, bb, :], ch=f"pp{bb}", deps=[e1, e2])
        ek.barrier()
        if phase != "F":
            if cfg.get("no_cc"):
                tcc = ek.dma(GPS, rs_dr[:, :], part_dr[0:TOK, :], ch="cc")
            else:
                tcc = ek.op(GPS, lambda eng: eng.collective_compute(
                    "ReduceScatter", ALU.add, [list(range(NC))],
                    [part_dr[:, :]], [rs_dr[:, :]]), n=16)
                ek.wait(GPS, tcc)
            ek.barrier()

        sproj = []
        xfu_prev = None
        x1u_prev = None
        fr_prev = None
        rsu = [None, None]
        for tt in range(TOK // 128):
            bb = tt % 2
            txf = ek.dma(SYN, xf2[:, 0, :],
                         xf_in[b][tt * 128:(tt + 1) * 128, :], ch="xf0",
                         deps=[xfu_prev])
            if phase == "F":
                mm_buf = [atu[0], atu[1]]
                mmt = None
                for rk in range(8):
                    cb = rk % 2
                    tld = ek.dma(SYN, at4[:, cb, :],
                                 htok_in[b][tt * 128:(tt + 1) * 128,
                                            rk * 128:(rk + 1) * 128],
                                 ch=f"at{cb}", deps=[mm_buf[cb]], transpose=True)
                    for dh in range(2):
                        mmt = ek.op(TENS, lambda eng, rk=rk, cb=cb, dh=dh: eng.matmul(
                            (p_a if dh == 0 else p_b)[:, 0:512],
                            at4[:, cb, :], wof3[:, rk, dh * 512:(dh + 1) * 512],
                            start=(rk == 0), stop=(rk == 7)),
                            deps=[tld, two, rsu[bb], sproj[-1] if sproj else None]
                            if rk == 0 and dh == 0 else [tld])
                    mm_buf[cb] = mmt
                atu[0], atu[1] = mm_buf
                e1 = ek.op(VEC, lambda eng, bb=bb: eng.tensor_copy(
                    out=fpp2[:, bb, 0:512], in_=p_a[:, 0:512]), deps=[mmt])
                trs0 = ek.op(VEC, lambda eng, bb=bb: eng.tensor_copy(
                    out=fpp2[:, bb, 512:1024], in_=p_b[:, 0:512]), deps=[mmt])
            else:
                trs0 = ek.dma(SYN, fpp2[:, bb, :],
                              rs_dr[tt * 128:(tt + 1) * 128, :], ch=f"rs{bb}",
                              deps=[rsu[bb]])
            t1 = ek.op(VEC, lambda eng, bb=bb: eng.tensor_tensor(
                f_r[:, 0:D], fpp2[:, bb, :], xf2[:, 0, :], op=ALU.add),
                deps=[txf, trs0, sproj[-1] if sproj else None, fr_prev])
            t2 = t1
            xfu_prev = t1
            rsu[bb] = t1
            sproj.append(t1)
            ts1 = ek.op(SCA, lambda eng: eng.activation(
                f_y[:, 0:D], f_r[:, 0:D], AF.Copy, accum_out=f_sm[:, 0:1]),
                deps=[t1, t2])
            ts2 = ek.op(SCA, lambda eng: eng.activation(
                f_y[:, 0:D], f_r[:, 0:D], AF.Square, accum_out=f_sm[:, 1:2]),
                deps=[ts1])
            tmu = ek.op(VEC, lambda eng: eng.tensor_scalar(
                f_sm[:, 2:3], f_sm[:, 0:1], 1.0 / D, None, op0=ALU.mult), deps=[ts1])
            tm2 = ek.op(VEC, lambda eng: eng.tensor_scalar(
                f_sm[:, 3:4], f_sm[:, 2:3], None, None, op0=ALU.mult,
                scalar1=None), deps=[tmu]) if False else None
            tm2 = ek.op(VEC, lambda eng: eng.tensor_tensor(
                f_sm[:, 3:4], f_sm[:, 2:3], f_sm[:, 2:3], op=ALU.mult), deps=[tmu])
            tvar = ek.op(VEC, lambda eng: eng.tensor_scalar(
                f_sm[:, 4:5], f_sm[:, 1:2], 1.0 / D, None, op0=ALU.mult), deps=[ts2])
            tvar2 = ek.op(VEC, lambda eng: eng.tensor_tensor(
                f_sm[:, 5:6], f_sm[:, 4:5], f_sm[:, 3:4], op=ALU.subtract),
                deps=[tvar, tm2])
            tsd = ek.op(SCA, lambda eng: eng.activation(
                f_sm[:, 6:7], f_sm[:, 5:6], AF.Sqrt, bias=c_zero4[:, 5:6]),
                deps=[tvar2])
            trs = ek.op(VEC, lambda eng: eng.reciprocal(
                f_sm[:, 7:8], f_sm[:, 6:7]), deps=[tsd])
            tx1 = ek.op(VEC, lambda eng, bb=bb: eng.tensor_scalar(
                x1s2[:, 0, :], f_r[:, 0:D], f_sm[:, 2:3], f_sm[:, 7:8],
                op0=ALU.subtract, op1=ALU.mult), deps=[trs, x1u_prev])
            fr_prev = tx1
            tst1 = ek.dma(SCA, x1_dr[tt * 128:(tt + 1) * 128, :], x1s2[:, 0, :],
                          ch="x10", deps=[tx1])
            x1u_prev = tst1
        ek.barrier()
        if dev:
            ek.dma(GPS, taps["x1"][:, :], x1_dr[:, :], ch="tap")
        if cfg.get("upto", "full") == "F1":
            ek.barrier()
            ar.pop()
            return

        smat_s1 = c_smat[:, 0:32]
        smat_s2 = c_smat[:, 32:64]
        smat_h1 = c_smat[:, 64:96]
        smat_h2 = c_smat[:, 96:128]
        smat_s3 = c_smat[:, 128:136]
        NT2 = TSL // 512
        for sl in range(NSL):
            t0 = sl * TSL
            tx1T = []
            for k in range(8):
                tx1T.append(ek.dma(SYN, x1T3[:, k, :],
                                   x1_dr[t0:t0 + TSL, k * 128:(k + 1) * 128],
                                   ch="x1T", transpose=True))
            wpu = [None, None]
            hdone = []
            for hm in range(32):
                wb = hm % 2
                tw1 = ek.dma(GPS, wp3[:, wb, 0:8, :],
                             w1T_d.rearrange("(k p) c -> p k c", p=128)
                             [:, :, hm * 128:(hm + 1) * 128], ch=f"w1{wb}",
                             deps=[wpu[wb]])
                mm1 = None
                for k in range(8):
                    for ts2_ in range(NT2):
                        mm1 = ek.op(TENS, lambda eng, k=k, ts2_=ts2_, wb=wb: eng.matmul(
                            (p_c if ts2_ == 0 else p_d)[:, 0:512],
                            wp3[:, wb, k, :], x1T3[:, k, ts2_ * 512:(ts2_ + 1) * 512],
                            start=(k == 0), stop=(k == 7)),
                            deps=[tw1, tx1T[-1], hdone[-1] if hdone else None]
                            if k == 0 and ts2_ == 0 else ())
                z1 = []
                for ts2_ in range(NT2):
                    z1.append(ek.op(SCA, lambda eng, hm=hm, ts2_=ts2_: eng.activation(
                        f_z[:, ts2_ * 512:(ts2_ + 1) * 512],
                        (p_c if ts2_ == 0 else p_d)[:, 0:512], AF.Silu,
                        scale=smat_s1[:, hm:hm + 1]), deps=[mm1]))
                tw2 = ek.dma(GPS, wp3[:, wb, 8:16, :],
                             w2T_d.rearrange("(k p) c -> p k c", p=128)
                             [:, :, hm * 128:(hm + 1) * 128], ch=f"w2{wb}",
                             deps=[wpu[wb]])
                mm2 = None
                for k in range(8):
                    for ts2_ in range(NT2):
                        mm2 = ek.op(TENS, lambda eng, k=k, ts2_=ts2_, wb=wb: eng.matmul(
                            (p_c if ts2_ == 0 else p_d)[:, 0:512],
                            wp3[:, wb, 8 + k, :], x1T3[:, k, ts2_ * 512:(ts2_ + 1) * 512],
                            start=(k == 0), stop=(k == 7)),
                            deps=[tw2] + z1 if k == 0 and ts2_ == 0 else ())
                z2 = []
                for ts2_ in range(NT2):
                    z2.append(ek.op(SCA, lambda eng, hm=hm, ts2_=ts2_: eng.activation(
                        f_z[:, 1024 + ts2_ * 512:1024 + (ts2_ + 1) * 512],
                        (p_c if ts2_ == 0 else p_d)[:, 0:512], AF.Copy,
                        scale=smat_s2[:, hm:hm + 1]), deps=[mm2]))
                th = ek.op(VEC, lambda eng, hm=hm: eng.tensor_tensor(
                    h3[:, hm, :], f_z[:, 0:TSL], f_z[:, 1024:1024 + TSL],
                    op=ALU.mult), deps=z1 + z2)
                hdone.append(th)
                wpu[wb] = mm2
            ek.barrier()
            sdone_prev = hdone[-1]
            for stage, (wt_d, sc, src3, dst3) in enumerate(
                    ((h1T_d, smat_h1, h3, h23), (h2T_d, smat_h2, h23, h3))):
                if stage == 1:
                    ek.barrier()
                wpu2 = [None, None]
                sdone = []
                for hm in range(32):
                    wb = hm % 2
                    twp = ek.dma(GPS, wp3[:, wb, :, :],
                                 wt_d.rearrange("(k p) c -> p k c", p=128)
                                 [:, :, hm * 128:(hm + 1) * 128], ch=f"h{stage}{wb}",
                                 deps=[wpu2[wb]])
                    mmh = None
                    for k in range(32):
                        for ts2_ in range(NT2):
                            mmh = ek.op(TENS, lambda eng, k=k, ts2_=ts2_, wb=wb, src3=src3: eng.matmul(
                                (p_c if ts2_ == 0 else p_d)[:, 0:512],
                                wp3[:, wb, k, :], src3[:, k, ts2_ * 512:(ts2_ + 1) * 512],
                                start=(k == 0), stop=(k == 31)),
                                deps=[twp, sdone_prev,
                                      sdone[-1] if sdone else None]
                                if k == 0 and ts2_ == 0 else ())
                    gouts = []
                    for ts2_ in range(NT2):
                        gouts.append(ek.op(SCA, lambda eng, hm=hm, ts2_=ts2_, dst3=dst3, sc=sc: eng.activation(
                            dst3[:, hm, ts2_ * 512:(ts2_ + 1) * 512],
                            (p_c if ts2_ == 0 else p_d)[:, 0:512], AF.Gelu,
                            scale=sc[:, hm:hm + 1]), deps=[mmh]))
                    sdone.append(gouts[-1])
                    wpu2[wb] = gouts[-1]
                sdone_prev = sdone[-1]
            ek.barrier()
            w3u = [None, None]
            ydone = []
            for dm in range(8):
                wb = dm % 2
                tw3 = ek.dma(GPS, wp3[:, wb, :, :],
                             w3T_d.rearrange("(k p) c -> p k c", p=128)
                             [:, :, dm * 128:(dm + 1) * 128], ch=f"w3{wb}",
                             deps=[w3u[wb]])
                mm3 = None
                for k in range(32):
                    for ts2_ in range(NT2):
                        mm3 = ek.op(TENS, lambda eng, k=k, ts2_=ts2_, wb=wb: eng.matmul(
                            (p_c if ts2_ == 0 else p_d)[:, 0:512],
                            wp3[:, wb, k, :], h3[:, k, ts2_ * 512:(ts2_ + 1) * 512],
                            start=(k == 0), stop=(k == 31)),
                            deps=[tw3, sdone_prev, ydone[-1] if ydone else None]
                            if k == 0 and ts2_ == 0 else ())
                for ts2_ in range(NT2):
                    ydone.append(ek.op(SCA, lambda eng, dm=dm, ts2_=ts2_: eng.activation(
                        yT3[:, dm, ts2_ * 512:(ts2_ + 1) * 512],
                        (p_c if ts2_ == 0 else p_d)[:, 0:512], AF.Copy,
                        scale=smat_s3[:, dm:dm + 1]), deps=[mm3]))
                w3u[wb] = ydone[-1]
            ek.barrier()
            x1ld_prev = None
            x1lu_prev = None
            ptk = None
            for t128 in range(TSL // 128):
                bb = t128 % 2
                gt = t0 + t128 * 128
                x1ld_prev = ek.dma(SYN, x1s2[:, 0, :],
                                   x1_dr[gt:gt + 128, :], ch="x1r",
                                   deps=[x1lu_prev])
                radd = []
                for dk in range(8):
                    tp = ek.op(TENS, lambda eng, dk=dk, t128=t128: eng.transpose(
                        p_t.bitcast(BF16)[:, 0:128],
                        yT3[:, dk, t128 * 128:(t128 + 1) * 128], c_identb),
                        deps=[ydone[-1], ptk])
                    ptk = ek.op(VEC, lambda eng, dk=dk: eng.tensor_copy(
                        out=f_y[:, dk * 128:(dk + 1) * 128],
                        in_=p_t.bitcast(BF16)[:, 0:128]), deps=[tp])
                    radd.append(ptk)
                t1 = ek.op(VEC, lambda eng, bb=bb: eng.tensor_tensor(
                    f_r[:, 0:D], f_y[:, 0:D], x1s2[:, 0, :], op=ALU.add),
                    deps=radd + [x1ld_prev])
                x1lu_prev = t1
                ts1 = ek.op(SCA, lambda eng: eng.activation(
                    f_y[:, 0:D], f_r[:, 0:D], AF.Copy, accum_out=f_sm[:, 0:1]),
                    deps=[t1] + radd)
                ts2b = ek.op(SCA, lambda eng: eng.activation(
                    f_y[:, 0:D], f_r[:, 0:D], AF.Square, accum_out=f_sm[:, 1:2]),
                    deps=[ts1])
                tmu = ek.op(VEC, lambda eng: eng.tensor_scalar(
                    f_sm[:, 2:3], f_sm[:, 0:1], 1.0 / D, None, op0=ALU.mult), deps=[ts1])
                tm2 = ek.op(VEC, lambda eng: eng.tensor_tensor(
                    f_sm[:, 3:4], f_sm[:, 2:3], f_sm[:, 2:3], op=ALU.mult), deps=[tmu])
                tvar = ek.op(VEC, lambda eng: eng.tensor_scalar(
                    f_sm[:, 4:5], f_sm[:, 1:2], 1.0 / D, None, op0=ALU.mult), deps=[ts2b])
                tvar2 = ek.op(VEC, lambda eng: eng.tensor_tensor(
                    f_sm[:, 5:6], f_sm[:, 4:5], f_sm[:, 3:4], op=ALU.subtract),
                    deps=[tvar, tm2])
                tsd = ek.op(SCA, lambda eng: eng.activation(
                    f_sm[:, 6:7], f_sm[:, 5:6], AF.Sqrt, bias=c_zero4[:, 5:6]),
                    deps=[tvar2])
                trs = ek.op(VEC, lambda eng: eng.reciprocal(
                    f_sm[:, 7:8], f_sm[:, 6:7]), deps=[tsd])
                tyo = ek.op(VEC, lambda eng: eng.tensor_scalar(
                    f_r[:, 0:D], f_r[:, 0:D], f_sm[:, 2:3], f_sm[:, 7:8],
                    op0=ALU.subtract, op1=ALU.mult), deps=[trs, ts2b])
                tso = ek.dma(SCA, y_out[b][gt:gt + 128, :], f_r[:, 0:D],
                             ch="yo", deps=[tyo])
                ptk = tso
            ek.barrier()
        ek.barrier()
        ar.pop()

    for b in range(min(NB, 1) if dev else NB):
        emit_A(b)
        emit_F(b)

    blk = enter(nc.Block())
    ek.finalize(blk)
    for h in reversed(ctx):
        h.__exit__(None, None, None)
    return nc, taps


def _host_constants(cfg):
    S, D, NR, NBK, G = cfg["S"], cfg["D"], cfg["NR"], cfg["NBK"], cfg["G"]
    W = 2 * NBK
    c = {}
    c["iotaK"] = np.tile(np.arange(NBK, dtype=np.float32), (128, 1))
    c["iotaT"] = np.tile(np.arange(W, dtype=np.float16), (128, 2))
    c["ltri"] = np.triu(np.ones((128, 128), np.float32), 1).astype(NPBF)
    st = np.zeros((128, 256), np.float32)
    st[:, 128:] = 1.0
    c["stair"] = st.astype(NPBF)
    c["onesc"] = np.ones((128, 1), NPBF)
    c["ones1"] = np.ones((1, 128), np.float32)
    c["identb"] = np.eye(128, dtype=np.float32).astype(NPBF)
    c["identh"] = np.eye(128, dtype=np.float32).astype(np.float16)
    ih = np.zeros((128, 64), np.float16)
    ih[64:, :] = np.eye(64, dtype=np.float16)
    c["identhh"] = ih
    c["identb2"] = np.tile(np.eye(64, dtype=np.float32), (2, 1)).astype(NPBF)
    z = np.zeros((128, 8), np.float32)
    z[:, 4] = 1e-6
    z[:, 5] = 1e-5
    c["zero4"] = z
    return c


def _prep_core_inputs(cfg, inputs, core):
    S, D, NR, NBK = cfg["S"], cfg["D"], cfg["NR"], cfg["NBK"]
    NC, TOK, NB = cfg["NC"], cfg["TOK"], cfg["NB"]
    HD = 4 * D
    import jax
    x = np.ascontiguousarray(np.asarray(inputs["L0_attn_x"], np.float32))
    h0 = 2 * core
    m = {}
    for b in range(NB):
        m[f"x{b}"] = x[b]
        m[f"xf{b}"] = np.ascontiguousarray(x[b][core * TOK:(core + 1) * TOK])
    wqk = np.asarray(inputs["L0_attn_w_qk"], np.float32)
    wv = np.asarray(inputs["L0_attn_w_v"], np.float32)
    wout = np.asarray(inputs["L0_attn_w_out"], np.float32)
    m["wqkT"] = np.ascontiguousarray(wqk[h0 * 64:(h0 + 2) * 64, :].T)
    m["wvT"] = np.ascontiguousarray(wv[h0 * 64:(h0 + 2) * 64, :].T)
    m["woutT"] = np.ascontiguousarray(wout.T[core * 128:(core + 1) * 128, :])
    rot = _rot_table(cfg)
    m["rot2"] = np.ascontiguousarray(
        rot[h0:h0 + 2].reshape(2, 64, NR * NBK).reshape(128, NR * NBK))
    m["w1T"] = np.ascontiguousarray(np.asarray(inputs["L0_ffn_w1_w"], np.float32).T)
    m["w2T"] = np.ascontiguousarray(np.asarray(inputs["L0_ffn_w2_w"], np.float32).T)
    m["h1T"] = np.ascontiguousarray(np.asarray(inputs["L0_ffn_h1_w"], np.float32).T)
    m["h2T"] = np.ascontiguousarray(np.asarray(inputs["L0_ffn_h2_w"], np.float32).T)
    m["w3T"] = np.ascontiguousarray(np.asarray(inputs["L0_ffn_w3_w"], np.float32).T)
    sm = np.zeros((128, 4 * (HD // 128) + D // 128), np.float32)
    col = 0
    for nm in ("L0_ffn_w1_s", "L0_ffn_w2_s", "L0_ffn_h1_s", "L0_ffn_h2_s"):
        v = np.asarray(inputs[nm], np.float32).reshape(HD // 128, 128).T
        sm[:, col:col + HD // 128] = v
        col += HD // 128
    v = np.asarray(inputs["L0_ffn_w3_s"], np.float32).reshape(D // 128, 128).T
    sm[:, col:col + D // 128] = v
    m["smat"] = sm
    m.update(_host_constants(cfg))
    return m


_ROT_CACHE = {}
def _rot_table(cfg):
    key = (cfg["S"], cfg["NR"], cfg["NBK"])
    if key not in _ROT_CACHE:
        import jax
        with jax.default_device(jax.devices("cpu")[0]):
            r = jax.random.normal(jax.random.key(42),
                                  (16, 64, cfg["NR"], cfg["NBK"]), dtype="float32")
            _ROT_CACHE[key] = np.asarray(r)
    return _ROT_CACHE[key]


def kernel(**inputs):
    from concourse.library_overlay import lower_extended_insts
    cfg = make_cfg()
    NC, NB, TOK, D, S = cfg["NC"], cfg["NB"], cfg["TOK"], cfg["D"], cfg["S"]
    in_maps = [_prep_core_inputs(cfg, inputs, c) for c in range(NC)]
    cfgA = dict(cfg); cfgA["phase"] = "A"
    ncA, _ = build_program(cfgA)
    lower_extended_insts(ncA)
    resA = run_bass_kernel_spmd(ncA, in_maps, core_ids=list(range(NC)))
    heads = [np.concatenate([np.asarray(resA.results[c][f"ho{b}"]) for c in range(NC)],
                            axis=1) for b in range(NB)]
    cfgF = dict(cfg); cfgF["phase"] = "F"
    ncF, _ = build_program(cfgF)
    lower_extended_insts(ncF)
    wout = np.asarray(inputs["L0_attn_w_out"], np.float32)
    for c in range(NC):
        for b in range(NB):
            in_maps[c][f"htok{b}"] = np.ascontiguousarray(heads[b][c * TOK:(c + 1) * TOK])
        in_maps[c]["woutF"] = np.ascontiguousarray(wout.T)
    resF = run_bass_kernel_spmd(ncF, in_maps, core_ids=list(range(NC)))
    out = np.zeros((NB, S, D), np.float32)
    for c in range(NC):
        for b in range(NB):
            out[b, c * TOK:(c + 1) * TOK] = resF.results[c][f"y{b}"]
    return out


# version 05 (final)
# speedup vs baseline: 1.0274x; 1.0274x over previous
import numpy as np
import ml_dtypes
import concourse.bass as bass
import concourse.mybir as mybir
from concourse.bass_utils import run_bass_kernel_spmd

F32 = mybir.dt.float32
F16 = mybir.dt.float16
BF16 = mybir.dt.bfloat16
U32 = mybir.dt.uint32
U8 = mybir.dt.uint8
AF = mybir.ActivationFunctionType
ALU = mybir.AluOpType
AX = mybir.AxisListType
IOA = bass.IndirectOffsetOnAxis
NPBF = ml_dtypes.bfloat16

ROWW = 132
OROW = 68


def make_cfg(S=16384, NC=8, NB=2, dev_taps=False):
    return dict(S=S, D=1024, DH=64, NR=4,
                NBK=2 ** ((2 * (S // 64)).bit_length() - 1) // 2,
                G=S // 128, NC=NC, TOK=S // NC, NB=NB, dev_taps=dev_taps)


class EK:
    ENGINES = ("sync", "scalar", "tensor", "vector", "gpsimd")

    def __init__(self, nc):
        self.nc = nc
        self.prog = {e: [] for e in self.ENGINES}
        self.ctx = []
        self.sems = {}
        self.lastwait = {e: {} for e in self.ENGINES}
        for e in self.ENGINES:
            self.sems[e] = self._sem("sem_" + e)
        self.dmasems = {}
        self.bar = self._sem("bar")

    def _sem(self, name):
        h = self.nc.semaphore(name)
        s = h.__enter__()
        self.ctx.append(h)
        return Sem(s, name)

    def wait(self, e, tok):
        if tok is None:
            return
        sem, val = tok
        if val <= 0:
            return
        lw = self.lastwait[e]
        if lw.get(sem.name, 0) >= val:
            return
        lw[sem.name] = val
        self.prog[e].append(lambda eng, s=sem, v=val: eng.wait_ge(s.h, v))

    def op(self, e, fn, deps=(), n=1):
        for d in deps:
            self.wait(e, d)
        s = self.sems[e]
        s.val += n
        self.prog[e].append(lambda eng, fn=fn, s=s, n=n: fn(eng).then_inc(s.h, n))
        return (s, s.val)

    def opq(self, e, fn, deps=()):
        for d in deps:
            self.wait(e, d)
        self.prog[e].append(fn)
        return None

    def dmach(self, e, ch):
        key = (e, ch)
        if key not in self.dmasems:
            self.dmasems[key] = self._sem(f"dma_{e}_{ch}")
        return self.dmasems[key]

    def dma(self, e, out, in_, ch="c", deps=(), **kw):
        for d in deps:
            self.wait(e, d)
        s = self.dmach(e, ch)
        s.val += 16
        self.prog[e].append(
            lambda eng, o=out, i=in_, s=s, kw=kw: eng.dma_start(out=o, in_=i, **kw).then_inc(s.h, 16))
        return (s, s.val)

    def idma(self, out, in_, ch, deps=(), **kw):
        for d in deps:
            self.wait("gpsimd", d)
        s = self.dmach("gpsimd", ch)
        s.val += 16
        self.prog["gpsimd"].append(
            lambda eng, o=out, i=in_, s=s, kw=kw: eng.indirect_dma_start(
                out=o, out_offset=kw.get("out_offset"), in_=i,
                in_offset=kw.get("in_offset"),
                element_offset=kw.get("element_offset", 0)).then_inc(s.h, 16))
        return (s, s.val)

    def barrier(self):
        for (e, ch), s in self.dmasems.items():
            self.wait(e, (s, s.val))
        self.bar.val += 5
        t = self.bar.val
        for e in self.ENGINES:
            self.prog[e].append(lambda eng, s=self.bar, t=t: eng.nop().then_inc(s.h, 1))
            self.prog[e].append(lambda eng, s=self.bar, t=t: eng.wait_ge(s.h, t))
            self.lastwait[e] = {}

    def finalize(self, blk):
        for e in self.ENGINES:
            def runner(eng, e=e):
                for f in self.prog[e]:
                    f(eng)
            getattr(blk, e)(runner)


class Sem:
    def __init__(self, h, name):
        self.h = h
        self.val = 0
        self.name = name


class Arena:
    def __init__(self, tile_f32, total):
        self.t = tile_f32
        self.total = total
        self.off = 0
        self.marks = []
        self.peak = 0

    def take(self, n_f32, dtype=F32, align=1):
        self.off = (self.off + align - 1) // align * align
        o = self.off
        self.off += int(n_f32)
        self.peak = max(self.peak, self.off)
        assert self.off <= self.total, f"arena overflow {self.off} > {self.total}"
        sl = self.t[:, o:o + int(n_f32)]
        return sl if dtype == F32 else sl.bitcast(dtype)

    def push(self):
        self.marks.append(self.off)

    def pop(self):
        self.off = self.marks.pop()


def build_program(cfg):
    S, D, DH, NR, NBK, G = (cfg[k] for k in ("S", "D", "DH", "NR", "NBK", "G"))
    NC, TOK, NB = cfg["NC"], cfg["TOK"], cfg["NB"]
    W = 2 * NBK
    HD = 4 * D
    NCH = S // 128
    NBLK = NR * S // 128
    dev = cfg["dev_taps"]

    nc = bass.Bass("TRN2", target_bir_lowering=False)
    dt = nc.dram_tensor

    x_in = [dt(f"x{b}", [S, D], F32, kind="ExternalInput").ap() for b in range(NB)]
    xf_in = [dt(f"xf{b}", [TOK, D], F32, kind="ExternalInput").ap() for b in range(NB)]
    wqkT = dt("wqkT", [D, 128], F32, kind="ExternalInput").ap()
    wvT = dt("wvT", [D, 128], F32, kind="ExternalInput").ap()
    woutT_d = dt("woutT", [128, D], F32, kind="ExternalInput").ap()
    rot2_d = dt("rot2", [128, NR * NBK], F32, kind="ExternalInput").ap()
    w1T_d = dt("w1T", [D, HD], F32, kind="ExternalInput").ap()
    w2T_d = dt("w2T", [D, HD], F32, kind="ExternalInput").ap()
    h1T_d = dt("h1T", [HD, HD], F32, kind="ExternalInput").ap()
    h2T_d = dt("h2T", [HD, HD], F32, kind="ExternalInput").ap()
    w3T_d = dt("w3T", [HD, D], F32, kind="ExternalInput").ap()
    NSM = 4 * (HD // 128) + D // 128
    smat_d = dt("smat", [128, NSM], F32, kind="ExternalInput").ap()
    iotaK_d = dt("iotaK", [128, NBK], F32, kind="ExternalInput").ap()
    iotaT_d = dt("iotaT", [128, 2 * W], F16, kind="ExternalInput").ap()
    ltri_d = dt("ltri", [128, 128], BF16, kind="ExternalInput").ap()
    stair_d = dt("stair", [128, 256], BF16, kind="ExternalInput").ap()
    onesc_d = dt("onesc", [128, 1], BF16, kind="ExternalInput").ap()
    ones1_d = dt("ones1", [1, 128], F32, kind="ExternalInput").ap()
    identb_d = dt("identb", [128, 128], BF16, kind="ExternalInput").ap()
    identh_d = dt("identh", [128, 128], F16, kind="ExternalInput").ap()
    identhh_d = dt("identhh", [128, 64], F16, kind="ExternalInput").ap()
    identb2_d = dt("identb2", [128, 64], BF16, kind="ExternalInput").ap()
    zero4_d = dt("zero4", [128, 8], F32, kind="ExternalInput").ap()

    y_out = [dt(f"y{b}", [TOK, D], F32, kind="ExternalOutput").ap() for b in range(NB)]

    sorted_d = dt("sorted_d", [NR * S, ROWW], F16, kind="Internal").ap()
    osort_d = dt("osort_d", [NR * S, OROW], F16, kind="Internal").ap()
    a_dram = dt("a_dram", [G, W], F32, kind="Internal").ap()
    heads_dr = dt("heads_d", [S, 128], BF16, kind="Internal").ap()
    phase = cfg.get("phase", "fused")
    ho_out = []
    htok_in = []
    woutF_d = None
    if phase == "A":
        ho_out = [dt(f"ho{b}", [S, 128], BF16, kind="ExternalOutput").ap()
                  for b in range(NB)]
    if phase == "F":
        htok_in = [dt(f"htok{b}", [TOK, D], BF16, kind="ExternalInput").ap()
                   for b in range(NB)]
        woutF_d = dt("woutF", [D, D], F32, kind="ExternalInput").ap()
    x1_dr = dt("x1_d", [TOK, D], BF16, kind="Internal").ap()
    part_dr = dt("part_d", [S, D], F32, kind="Internal").ap()
    rs_dr = dt("rs_d", [TOK, D], F32, kind="Internal").ap()
    ag_dr = dt("ag_d", [NC * TOK, 128], BF16, kind="Internal").ap()

    taps = {}
    if dev:
        taps["qkT"] = dt("tap_qkT", [128, S], BF16, kind="ExternalOutput").ap()
        taps["vT"] = dt("tap_vT", [128, S], BF16, kind="ExternalOutput").ap()
        taps["bkt"] = dt("tap_bkt", [128, 2 * NR * G], F32, kind="ExternalOutput").ap()
        taps["dest"] = dt("tap_dest", [128, 2 * NR * G], F32, kind="ExternalOutput").ap()
        taps["srt"] = dt("tap_srt", [NR * S, ROWW], F16, kind="ExternalOutput").ap()
        taps["osrt"] = dt("tap_osrt", [NR * S, OROW], F16, kind="ExternalOutput").ap()
        taps["heads"] = dt("tap_heads", [S, 128], BF16, kind="ExternalOutput").ap()
        taps["x1"] = dt("tap_x1", [TOK, D], F32, kind="ExternalOutput").ap()

    ctx = []
    def enter(h):
        t = h.__enter__()
        ctx.append(h)
        return t

    AR_TOT = 51200
    arena = enter(nc.sbuf_tensor("arena", [128, AR_TOT], F32))
    ar = Arena(arena, AR_TOT)

    c_iotaK = ar.take(NBK)
    c_iotaT = ar.take(W, F16)
    c_ltri = ar.take(64, BF16)
    c_stair = ar.take(128, BF16)
    c_onesc = ar.take(4, BF16)[:, 0:1]
    c_ones1 = ar.take(128)[0:1, :]
    c_identb = ar.take(64, BF16)
    c_identh = ar.take(64, F16)
    c_identhh = ar.take(32, F16)
    c_identb2 = ar.take(32, BF16)
    c_zero4 = ar.take(8)
    c_rot2 = ar.take(NR * NBK // 2, BF16)
    c_wqkT = ar.take(D // 2, BF16)
    c_wvT = ar.take(D // 2, BF16)
    c_smat = ar.take(NSM)
    CONST_END = ar.off

    p_a = enter(nc.psum_tensor([128, 1024], F32))
    p_b = enter(nc.psum_tensor([128, 1024], F32))
    p_c = enter(nc.psum_tensor([128, 512], F32))
    p_d = enter(nc.psum_tensor([128, 512], F32))
    p_e = enter(nc.psum_tensor([128, 512], F32))
    p_t = enter(nc.psum_tensor([128, 1024], F16))

    ek = EK(nc)
    for h in ek.ctx:
        ctx.append(h)

    TENS, VEC, SCA, SYN, GPS = "tensor", "vector", "scalar", "sync", "gpsimd"

    t0 = []
    for dst, src in [(c_iotaK, iotaK_d), (c_iotaT, iotaT_d), (c_ltri, ltri_d),
                     (c_stair, stair_d), (c_onesc, onesc_d), (c_ones1, ones1_d),
                     (c_identb, identb_d), (c_identh, identh_d),
                     (c_identhh, identhh_d), (c_identb2, identb2_d), (c_zero4, zero4_d), (c_smat, smat_d)]:
        t0.append(ek.dma(SYN, dst, src[:]))
    t0.append(ek.dma(GPS, c_rot2, rot2_d[:]))
    t0.append(ek.dma(GPS, c_wqkT.rearrange("p (k n) -> p k n", n=128),
                     wqkT.rearrange("(k p) n -> p k n", p=128)))
    t0.append(ek.dma(GPS, c_wvT.rearrange("p (k n) -> p k n", n=128),
                     wvT.rearrange("(k p) n -> p k n", p=128)))
    ek.barrier()

    def emit_A(b):
        if phase == "F":
            return
        ar.push()
        a_qkT = ar.take(S // 2, BF16)
        a_vT = ar.take(S // 2, BF16)
        a_rv = ar.take(2 * NR * NBK)
        a_sm = ar.take(8 * NR)
        a_smu = ar.take(NR, U8)[:, 0:4]
        a_eq = ar.take(NR * NBK // 2, F16)
        a_eq2 = ar.take(NR * NBK // 2, F16)
        a_bkt = ar.take(NR * G)
        a_nsm = ar.take(8)
        a_o = ar.take(2 * 2 * W // 2, BF16)
        a_tot = ar.take(W)[0:1, :]
        a_tsh = ar.take(W + 4)[0:1, :]
        a_ones_row = ar.take(W)[0:1, :]
        a_start = ar.take(W)[0:1, :]
        a_amat = ar.take(W)
        a_ttmp = ar.take(W // 2, BF16)
        a_destF = ar.take(NR * G)
        a_destU = ar.take(NR * G, U32)
        R1 = ar.take(max(128 * (S // 128) * ROWW // 2 // 128, (S + 128) // 2))
        a_rowsF = R1.bitcast(F16)
        a_khT = R1.bitcast(F16)
        R2 = ar.take(8192)
        a_xrow = R2[:, 0:512].bitcast(BF16)
        a_xrow = R2[:, 0:1024].bitcast(BF16)
        a_xT = R2[:, 1024:3072].bitcast(BF16)
        a_aflat = R2[0:1, 0:8192]
        sub = Arena(R2, 8192)
        sub.off = 0
        a_stash = sub.take(NR * 64 // 2 + NR, F16)
        a_panel = sub.take(16 * ROWW // 2, F16)
        a_vwA = sub.take(2 * 8 * 68 // 2, F16)
        a_vwB = sub.take(2 * 8 * 68 // 2, F16)
        a_ptile = sub.take(192, F16)
        a_pt2 = sub.take(2 * 64, F16)
        a_asm = sub.take(32)
        a_orow = sub.take(8 * OROW // 2, F16)
        a_gath = sub.take(2 * NR * OROW // 2, F16)
        a_csm = sub.take(16)
        a_hrow = sub.take(4 * 128 // 2, BF16)

        UPTO = cfg.get("upto", "full")
        qkT3 = a_qkT.rearrange("p (n t) -> p n t", t=512)
        vT3 = a_vT.rearrange("p (n t) -> p n t", t=512)
        xT3 = a_xT.rearrange("p (k t) -> p k t", t=512)
        xrow2 = a_xrow.rearrange("p (n d) -> p n d", d=D)

        tok_qkT = [None] * (S // 512)
        tok_vT = [None] * (S // 512)
        xld = [None, None]
        xuse = [None, None]
        pc_free = None
        pd_free = None
        for tt in range(S // 512):
            trans_toks = []
            for q in range(4):
                buf = (tt * 4 + q) % 2
                xld[buf] = ek.dma(GPS, xrow2[:, buf, :],
                                  x_in[b][(tt * 4 + q) * 128:(tt * 4 + q) * 128 + 128, :],
                                  ch=f"x{buf}",
                                  deps=[xuse[buf]] if xuse[buf] else [])
                for k in range(8):
                    tk = ek.op(TENS, lambda eng, buf=buf, k=k: eng.transpose(
                        p_t.bitcast(BF16)[:, 0:128],
                        xrow2[:, buf, k * 128:(k + 1) * 128], c_identb),
                        deps=[xld[buf], trans_toks[-1] if trans_toks else None])
                    ev = ek.op(VEC, lambda eng, k=k, q=q: eng.tensor_copy(
                        out=xT3[:, k, q * 128:(q + 1) * 128],
                        in_=p_t.bitcast(BF16)[:, 0:128]), deps=[tk])
                    trans_toks.append(ev)
                xuse[buf] = trans_toks[-1]
            wq = c_wqkT.rearrange("p (k n) -> p k n", n=128)
            wv = c_wvT.rearrange("p (k n) -> p k n", n=128)
            for which, wmat, psum in ((0, wq, p_c), (1, wv, p_d)):
                mmt = None
                deps = list(trans_toks[-1:]) + [pc_free if which == 0 else pd_free]
                for k in range(8):
                    mmt = ek.op(TENS, lambda eng, wmat=wmat, k=k, psum=psum: eng.matmul(
                        psum[:], wmat[:, k, :], xT3[:, k, :],
                        start=(k == 0), stop=(k == 7)),
                        deps=deps if k == 0 else ())
                dstT = qkT3 if which == 0 else vT3
                ev = ek.op(VEC, lambda eng, dstT=dstT, tt=tt, psum=psum: eng.tensor_copy(
                    out=dstT[:, tt, :], in_=psum[:]), deps=[mmt])
                if which == 0:
                    tok_qkT[tt] = ev
                    pc_free = ev
                else:
                    tok_vT[tt] = ev
                    pd_free = ev
        tok_projdone = [tok_qkT[-1], tok_vT[-1]]

        if dev:
            ek.dma(SYN, taps["qkT"][:], a_qkT, ch="tap", deps=tok_projdone)
            ek.dma(SYN, taps["vT"][:], a_vT, ch="tap", deps=tok_projdone)
        ek.barrier()

        rows3 = a_rowsF[:, 0:NCH * ROWW].rearrange("p (g w) -> p g w", w=ROWW)

        for h2 in range(2):
            hb = h2 * 64
            prv = [p_a.rearrange("p (n w) -> p n w", w=NR * NBK)[:, 0, :],
                   p_b.rearrange("p (n w) -> p n w", w=NR * NBK)[:, 0, :]]
            rv2 = a_rv.rearrange("p (n w) -> p n w", w=NR * NBK)
            sm8 = a_sm.rearrange("p (k r) -> p k r", r=NR)
            bkt3 = a_bkt.rearrange("p (r g) -> p r g", g=G)
            rvfree = [None, None]
            ptfree = None
            rows_tok = [None] * NCH
            for g in range(NCH):
                buf = g % 2
                nmm = (NR * NBK + 511) // 512
                mmt = None
                for mi in range(nmm):
                    n0, n1 = mi * 512, min(NR * NBK, (mi + 1) * 512)
                    mmt = ek.op(TENS, lambda eng, g=g, buf=buf, n0=n0, n1=n1, hb=hb: eng.matmul(
                        prv[buf][:, n0:n1],
                        a_qkT[hb:hb + 64, g * 128:(g + 1) * 128],
                        c_rot2[hb:hb + 64, n0:n1], start=True, stop=True),
                        deps=[tok_projdone[0], rvfree[buf]] if mi == 0 else ())
                tq = ek.op(TENS, lambda eng, g=g, hb=hb: eng.transpose(
                    p_t.bitcast(BF16)[:, 0:64],
                    a_qkT[hb:hb + 64, g * 128:(g + 1) * 128], c_identb2[hb:hb + 64, :]),
                    deps=[ptfree, tok_projdone[1]])
                tv = ek.op(TENS, lambda eng, g=g, hb=hb: eng.transpose(
                    p_t.bitcast(BF16)[:, 64:128],
                    a_vT[hb:hb + 64, g * 128:(g + 1) * 128], c_identb2[hb:hb + 64, :]),
                    deps=())
                tsq = ek.op(SCA, lambda eng: eng.activation(
                    a_ttmp.bitcast(F16)[:, 0:64], p_t.bitcast(BF16)[:, 0:64],
                    AF.Square, accum_out=a_nsm[:, 0:1]), deps=[tq])
                tsm = ek.op(SCA, lambda eng: eng.activation(
                    a_nsm[:, 1:2], a_nsm[:, 0:1], AF.Sqrt,
                    scale=1.0 / 64.0, bias=c_zero4[:, 4:5]), deps=[tsq])
                tc = ek.op(SCA, lambda eng: eng.activation(
                    a_nsm[:, 2:3], a_nsm[:, 1:2], AF.Copy, scale=8.0), deps=[tsm])
                trc = ek.op(VEC, lambda eng: eng.reciprocal(
                    a_nsm[:, 3:4], a_nsm[:, 1:2]), deps=[tsm])
                tsc = ek.op(VEC, lambda eng: eng.tensor_scalar(
                    a_nsm[:, 4:5], a_nsm[:, 3:4], 0.125, None, op0=ALU.mult),
                    deps=[trc])
                tk1 = ek.op(VEC, lambda eng, g=g: eng.tensor_scalar(
                    rows3[:, g, 0:64], p_t.bitcast(BF16)[:, 0:64],
                    a_nsm[:, 4:5], None, op0=ALU.mult), deps=[tsc, tq])
                tk2 = ek.op(VEC, lambda eng, g=g: eng.tensor_copy(
                    out=rows3[:, g, 64:65], in_=a_nsm[:, 2:3]), deps=[tc])
                tk3 = ek.op(VEC, lambda eng, g=g: eng.tensor_copy(
                    out=rows3[:, g, 68:132], in_=p_t.bitcast(BF16)[:, 64:128]),
                    deps=[tv])
                tk4 = ek.op(VEC, lambda eng, g=g: eng.tensor_copy(
                    out=rows3[:, g, 65:68], in_=c_zero4[:, 0:3]), deps=())
                ptfree = tk3
                rows_tok[g] = (tk1, tk2, tk3, tk4)
                rvp3 = prv[buf].rearrange("p (r k) -> p r k", k=NBK)
                tmx = ek.op(VEC, lambda eng, rvp3=rvp3: eng.tensor_reduce(
                    sm8[:, 0, :], rvp3, axis=AX.X, op=ALU.max), deps=[mmt])
                tmn = ek.op(VEC, lambda eng, rvp3=rvp3: eng.tensor_reduce(
                    sm8[:, 1, :], rvp3, axis=AX.X, op=ALU.min), deps=[mmt])
                tev = ek.op(SCA, lambda eng, buf=buf: eng.activation(
                    rv2[:, buf, :], prv[buf], AF.Copy), deps=[mmt, tc])
                tsum = ek.op(VEC, lambda eng: eng.tensor_tensor(
                    sm8[:, 2, :], sm8[:, 0, :], sm8[:, 1, :], op=ALU.add),
                    deps=[tmx, tmn])
                tps = ek.op(VEC, lambda eng: eng.tensor_tensor(
                    a_smu[:, 0:NR], sm8[:, 2, :], c_zero4[:, 0:NR], op=ALU.is_ge),
                    deps=[tsum])
                tsel = ek.op(VEC, lambda eng: eng.select(
                    sm8[:, 3, :], a_smu[:, 0:NR], sm8[:, 0, :], sm8[:, 1, :],
                    add_drain=True), deps=[tps], n=2)
                toff = ek.op(VEC, lambda eng: eng.tensor_scalar(
                    sm8[:, 4, :], a_smu[:, 0:NR], -float(NBK), float(NBK),
                    op0=ALU.mult, op1=ALU.add), deps=[tps])
                eq3 = a_eq.rearrange("p (r k) -> p r k", k=NBK)
                eq23 = a_eq2.rearrange("p (r k) -> p r k", k=NBK)
                selb = sm8[:, 3, :].unsqueeze(2).broadcast_to((128, NR, NBK))
                teq = ek.op(VEC, lambda eng, buf=buf, selb=selb, eq3=eq3: eng.tensor_tensor(
                    eq3, rv2[:, buf, :].rearrange("p (r k) -> p r k", k=NBK), selb,
                    op=ALU.is_equal), deps=[tsel, tev])
                iob = c_iotaK.unsqueeze(1).broadcast_to((128, NR, NBK))
                tdot = ek.op(VEC, lambda eng, eq3=eq3, eq23=eq23, iob=iob: eng.tensor_tensor(
                    eq23, eq3, iob, op=ALU.mult), deps=[teq])
                rvfree[buf] = tdot
                tred = ek.op(VEC, lambda eng, eq23=eq23, g=g: eng.tensor_reduce(
                    bkt3[:, :, g], eq23, axis=AX.X, op=ALU.add), deps=[tdot])
                tbk = ek.op(VEC, lambda eng, g=g: eng.tensor_tensor(
                    bkt3[:, :, g], bkt3[:, :, g], sm8[:, 4, :], op=ALU.add),
                    deps=[tred, toff])
                bkt_done = tbk

            if dev:
                ek.dma(SYN, taps["bkt"][:, h2 * NR * G:(h2 + 1) * NR * G],
                       a_bkt, ch="tap", deps=[bkt_done])

            if UPTO == "A2":
                ek.barrier()
                continue
            o2 = a_o.rearrange("p (n w) -> p n w", w=W)
            destF3 = a_destF.rearrange("p (r g) -> p r g", g=G)
            destU3 = a_destU.rearrange("p (r g) -> p r g", g=G)
            scat_toks = []
            prev_ab = []
            prev_afl_use = []
            for r in range(NR):
                ofree = [None, None]
                pmm = None
                tmm = None
                for g in range(G):
                    ob = g % 2
                    to = ek.op(VEC, lambda eng, r=r, g=g, ob=ob: eng.tensor_scalar(
                        o2[:, ob, :], c_iotaT[:, 0:W], bkt3[:, r, g:g + 1], None,
                        op0=ALU.is_equal), deps=[bkt_done, ofree[ob]])
                    pmm = ek.op(TENS, lambda eng, g=g, ob=ob: eng.matmul(
                        p_c[:, 0:W], c_stair[:, 127 - g:255 - g], o2[:, ob, :],
                        start=(g == 0), stop=(g == G - 1)), deps=[to])
                    tmm = ek.op(TENS, lambda eng, g=g, ob=ob: eng.matmul(
                        p_d[0:1, 0:W], c_onesc[:], o2[:, ob, :],
                        start=(g == 0), stop=(g == G - 1)), deps=())
                    ofree[ob] = tmm
                tte = ek.op(VEC, lambda eng: eng.tensor_copy(
                    out=a_tot[:], in_=p_d[0:1, 0:W]), deps=[tmm])
                tz = ek.op(VEC, lambda eng: eng.tensor_scalar(
                    a_tsh[0:1, 0:1], a_tot[0:1, 0:1], 0.0, None, op0=ALU.mult),
                    deps=[tte])
                tsh2 = ek.op(VEC, lambda eng: eng.tensor_copy(
                    out=a_tsh[0:1, 1:W], in_=a_tot[0:1, 0:W - 1]), deps=[tte])
                tone = ek.op(VEC, lambda eng: eng.tensor_scalar(
                    a_ones_row[:], a_tot[:], 0.0, 1.0, op0=ALU.mult, op1=ALU.add),
                    deps=[tte])
                tsc2 = ek.op(VEC, lambda eng: eng.tensor_tensor_scan(
                    a_start[:], a_ones_row[:], a_tsh[0:1, 0:W], 0.0,
                    op0=ALU.mult, op1=ALU.add), deps=[tz, tsh2, tone])
                fmm = ek.op(TENS, lambda eng: eng.matmul(
                    p_c[:, 0:W], c_ones1[:], a_start[:], start=False, stop=True,
                    skip_group_check=True), deps=[tsc2, pmm])
                tam = ek.op(SCA, lambda eng: eng.activation(
                    a_amat[:, 0:W], p_c[:, 0:W], AF.Copy),
                    deps=[fmm] + prev_ab)
                tad = ek.dma(SYN, a_dram[:, :], a_amat[0:G, 0:W], ch="ab",
                             deps=[tam] + prev_afl_use)
                NAB = 8
                afl = [None, None]
                afl_use = [None, None]
                rfree = [None, None]
                exq = None
                for g in range(G):
                    ab = (g // NAB) % 2
                    if g % NAB == 0:
                        afl[ab] = ek.dma(SYN, a_aflat[0:1, ab * NAB * W:(ab + 1) * NAB * W],
                                         a_dram.rearrange("g w -> (g w)")
                                         [g * W:(g + NAB) * W].unsqueeze(0),
                                         ch=f"af{ab}", deps=[tad, afl_use[ab]])
                    ob = g % 2
                    to = ek.op(VEC, lambda eng, r=r, g=g, ob=ob: eng.tensor_scalar(
                        o2[:, ob, :], c_iotaT[:, 0:W], bkt3[:, r, g:g + 1], None,
                        op0=ALU.is_equal), deps=[ofree[ob]])
                    pr = p_e if (g % 2 == 0) else p_c
                    rmm = ek.op(TENS, lambda eng, g=g, ob=ob, pr=pr: eng.matmul(
                        pr[:, 0:W], c_ltri[:, 0:128], o2[:, ob, :],
                        start=True, stop=False, skip_group_check=True),
                        deps=[to, rfree[g % 2], tam if g == 0 else None])
                    fm2 = ek.op(TENS, lambda eng, g=g, ab=ab, pr=pr: eng.matmul(
                        pr[:, 0:W], c_ones1[:],
                        a_aflat[0:1, (ab * NAB + (g % NAB)) * W:(ab * NAB + g % NAB + 1) * W],
                        start=False, stop=True, skip_group_check=True),
                        deps=[afl[ab]])
                    afl_use[ab] = fm2
                    tex = ek.op(VEC, lambda eng, r=r, g=g, ob=ob, pr=pr: eng.tensor_tensor_reduce(
                        a_ttmp[:, 0:W], pr[:, 0:W], o2[:, ob, :],
                        1.0, 0.0, op0=ALU.mult, op1=ALU.add,
                        accum_out=destF3[:, r, g:g + 1]), deps=[fm2])
                    ofree[ob] = tex
                    rfree[g % 2] = tex
                    exq = tex
                prev_ab = [tad]
                prev_afl_use = [t for t in afl_use if t]
                tcl = ek.op(VEC, lambda eng, r=r: eng.tensor_scalar(
                    destF3[:, r, :], destF3[:, r, :], float(S - 1), 0.0,
                    op0=ALU.min, op1=ALU.max), deps=[exq])
                tcu = ek.op(VEC, lambda eng, r=r: eng.tensor_copy(
                    out=destU3[:, r, :], in_=destF3[:, r, :]), deps=[tcl])
                for g in range(G):
                    stt = ek.idma(sorted_d, rows3[:, g, :], ch="scat",
                                  deps=[tcu] + list(rows_tok[g]),
                                  out_offset=IOA(ap=destU3[:, r, g:g + 1], axis=0),
                                  element_offset=r * S * ROWW)
                    scat_toks.append(stt)
            if dev:
                ek.dma(SYN, taps["dest"][:, h2 * NR * G:(h2 + 1) * NR * G],
                       a_destF, ch="tap", deps=[tcu])
            ek.barrier()
            if dev:
                ek.dma(SYN, taps["srt"][:, :], sorted_d[:, :], ch="tap")
                ek.barrier()

            if UPTO == "A3":
                continue
            pan3 = a_panel.rearrange("p (n w) -> p n w", w=8 * ROWW)
            orow3 = a_orow.rearrange("p (n w) -> p n w", w=OROW)
            srt4 = sorted_d.rearrange("(a blk p) w -> a p blk w", p=128, blk=8)
            pan4 = a_panel.rearrange("p (n blk w) -> p n blk w", blk=8, w=ROWW)
            tst = None
            tt1 = None
            for r in range(NR):
                tld = ek.dma(SYN, a_panel[0:64, 0:ROWW],
                             sorted_d[r * S + S - 64:r * S + S, :], ch="pan0",
                             deps=[tt1])
                tt1 = ek.op(TENS, lambda eng, r=r: eng.transpose(
                    p_t[0:65, 0:64], a_panel[0:64, 0:65], c_identh[0:64, 0:64]),
                    deps=[tld, tst])
                tst = ek.op(SCA, lambda eng, r=r: eng.activation(
                    a_stash[0:65, r * 64:(r + 1) * 64], p_t[0:65, 0:64], AF.Copy),
                    deps=[tt1])

            if UPTO == "A4a":
                ek.barrier()
                continue
            exq2 = [None, None]
            prev_vAB = [[tst], [tst]]
            pfree = [None, None]
            pl_tok = [None, None]
            st_prev = None
            for r in range(NR):
                kev = tst
                for pb in range(NCH // 8):
                    pbuf = pb % 2
                    tpl = ek.dma(SYN, pan4[:, pbuf], srt4[r * (NCH // 8) + pb],
                                 ch=f"pan{pbuf}",
                                 deps=[kev, pfree[0], pfree[1]] if pb == 0
                                 else [kev, pfree[pbuf]])
                    for j in range(8):
                        m = pb * 8 + j
                        tt1 = ek.op(TENS, lambda eng, pbuf=pbuf, j=j: eng.transpose(
                            p_t[0:65, 0:128], pan3[:, pbuf, j * ROWW:j * ROWW + 65],
                            c_identh[:, :]), deps=[tpl, kev])
                        kev = ek.op(SCA, lambda eng, m=m: eng.activation(
                            a_khT[0:65, m * 128:(m + 1) * 128], p_t[0:65, 0:128],
                            AF.Copy), deps=[tt1])
                    pfree[pbuf] = kev
                vwA4 = a_vwA.rearrange("p (n blk w) -> p n blk w", blk=8, w=68)
                vwB4 = a_vwB.rearrange("p (n blk w) -> p n blk w", blk=8, w=68)
                if UPTO == "A4b":
                    continue
                for pb in range(NCH // 8):
                    pbuf = pb % 2
                    tB = ek.dma(SYN, vwB4[:, pbuf],
                                sorted_d[r * S + pb * 1024:r * S + (pb + 1) * 1024, 64:132]
                                .rearrange("(blk p) w -> p blk w", p=128),
                                ch=f"vb{pbuf}", deps=[pfree[pbuf], kev])
                    if pb == 0:
                        tA0 = ek.dma(SYN, vwA4[0:64, pbuf, 0],
                                     sorted_d[((r - 1) % NR) * S + S - 64:
                                              ((r - 1) % NR) * S + S, 64:132],
                                     ch=f"va{pbuf}", deps=[pfree[pbuf], kev])
                        tA1 = ek.dma(SYN, vwA4[64:128, pbuf, 0],
                                     sorted_d[r * S:r * S + 64, 64:132],
                                     ch=f"va{pbuf}", deps=[])
                        tA = ek.dma(SYN, vwA4[:, pbuf, 1:8].rearrange("p blk w -> p blk w"),
                                    sorted_d[r * S + 64:r * S + 64 + 896, 64:132]
                                    .rearrange("(blk p) w -> p blk w", p=128),
                                    ch=f"va{pbuf}", deps=[])
                    else:
                        tA = ek.dma(SYN, vwA4[:, pbuf],
                                    sorted_d[r * S + pb * 1024 - 64:
                                             r * S + (pb + 1) * 1024 - 64, 64:132]
                                    .rearrange("(blk p) w -> p blk w", p=128),
                                    ch=f"va{pbuf}", deps=[pfree[pbuf], kev])
                    pl_tok[pbuf] = tA if pb > 0 else tA
                    plB = tB
                    for j in range(8):
                        m = pb * 8 + j
                        q = m % 2
                        pdq = p_c if q == 0 else p_e
                        asm = a_asm[:, 8 * q:8 * q + 8]
                        ptl = a_ptile[:, 192 * q:192 * q + 192]
                        pt2 = a_pt2[:, 128 * q:128 * q + 128]
                        ptt = p_t[:, 256 * q:256 * q + 128]
                        pdd = p_d[:, 64 * q:64 * q + 64]
                        d1 = ek.op(TENS, lambda eng, m=m, pdq=pdq: eng.matmul(
                            pdq[:, 64:192], a_khT[0:64, m * 128:m * 128 + 128],
                            a_khT[0:64, m * 128:m * 128 + 128],
                            start=True, stop=True), deps=[kev, exq2[q]])
                        if m > 0:
                            d2 = ek.op(TENS, lambda eng, m=m, pdq=pdq: eng.matmul(
                                pdq[:, 0:64], a_khT[0:64, m * 128:m * 128 + 128],
                                a_khT[0:64, m * 128 - 64:m * 128],
                                start=True, stop=True), deps=())
                        else:
                            d2 = ek.op(TENS, lambda eng, r=r, pdq=pdq: eng.matmul(
                                pdq[:, 0:64], a_khT[0:64, 0:128],
                                a_stash[0:64, ((r - 1) % NR) * 64:((r - 1) % NR) * 64 + 64],
                                start=True, stop=True), deps=[tst])
                        tc1 = ek.op(VEC, lambda eng, pbuf=pbuf, j=j, asm=asm: eng.tensor_copy(
                            out=asm[:, 0:1], in_=vwB4[:, pbuf, j, 0:1]),
                            deps=[plB, exq2[q]])
                        mA = ek.op(VEC, lambda eng, pdq=pdq, asm=asm: eng.tensor_reduce(
                            asm[0:64, 1:2], pdq[0:64, 0:128], axis=AX.X,
                            op=ALU.max), deps=[d1, d2])
                        mB = ek.op(VEC, lambda eng, pdq=pdq, asm=asm: eng.tensor_reduce(
                            asm[64:128, 1:2], pdq[64:128, 64:192], axis=AX.X,
                            op=ALU.max), deps=[d1])
                        tb = ek.op(VEC, lambda eng, asm=asm: eng.tensor_tensor(
                            asm[:, 2:3], asm[:, 0:1], asm[:, 1:2], op=ALU.mult),
                            deps=[mA, mB, tc1])
                        tnb = ek.op(VEC, lambda eng, asm=asm: eng.tensor_scalar(
                            asm[:, 3:4], asm[:, 2:3], -1.0, None, op0=ALU.mult),
                            deps=[tb])
                        eA = ek.op(SCA, lambda eng, pdq=pdq, asm=asm, ptl=ptl: eng.activation(
                            ptl[0:64, 0:128], pdq[0:64, 0:128],
                            AF.Exp, bias=asm[0:64, 3:4], scale=asm[0:64, 0:1],
                            accum_out=asm[0:64, 4:5]), deps=[tnb] + prev_vAB[q])
                        eB = ek.op(SCA, lambda eng, pdq=pdq, asm=asm, ptl=ptl: eng.activation(
                            ptl[64:128, 64:192], pdq[64:128, 64:192],
                            AF.Exp, bias=asm[64:128, 3:4], scale=asm[64:128, 0:1],
                            accum_out=asm[64:128, 4:5]), deps=[tnb])
                        tln = ek.op(SCA, lambda eng, asm=asm: eng.activation(
                            asm[:, 5:6], asm[:, 4:5], AF.Ln), deps=[eA, eB])
                        tlg = ek.op(VEC, lambda eng, asm=asm: eng.tensor_tensor(
                            asm[:, 6:7], asm[:, 5:6], asm[:, 2:3], op=ALU.add),
                            deps=[tln])
                        pA1 = ek.op(TENS, lambda eng, ptl=ptl, ptt=ptt: eng.transpose(
                            ptt[:, 0:64], ptl[0:64, 0:128],
                            c_identh[0:64, 0:64]), deps=[eA])
                        pB = ek.op(TENS, lambda eng, ptl=ptl, ptt=ptt: eng.transpose(
                            ptt[:, 64:128], ptl[64:128, 64:192],
                            c_identhh[64:128, :]), deps=[eB])
                        vA = ek.op(SCA, lambda eng, pt2=pt2, ptt=ptt: eng.activation(
                            pt2[:, 0:64], ptt[:, 0:64], AF.Copy), deps=[pA1])
                        vB = ek.op(SCA, lambda eng, pt2=pt2, ptt=ptt: eng.activation(
                            pt2[:, 64:128], ptt[:, 64:128], AF.Copy), deps=[pB])
                        prev_vAB[q] = [vA, vB]
                        a2 = ek.op(TENS, lambda eng, pbuf=pbuf, j=j, pt2=pt2, pdd=pdd: eng.matmul(
                            pdd[0:64, :], pt2[:, 0:64], vwA4[:, pbuf, j, 4:68],
                            start=True, stop=True), deps=[vA, pl_tok[pbuf]])
                        a4 = ek.op(TENS, lambda eng, pbuf=pbuf, j=j, pt2=pt2, pdd=pdd: eng.matmul(
                            pdd[64:128, :], pt2[:, 64:128], vwB4[:, pbuf, j, 4:68],
                            start=True, stop=True), deps=[vB, plB])
                        rc = ek.op(VEC, lambda eng, asm=asm: eng.reciprocal(
                            asm[:, 7:8], asm[:, 4:5]), deps=[eA, eB])
                        ob2 = m % 8
                        no = ek.op(VEC, lambda eng, ob2=ob2, asm=asm, pdd=pdd: eng.tensor_scalar(
                            orow3[:, ob2, 0:64], pdd[:, :], asm[:, 7:8], None,
                            op0=ALU.mult),
                            deps=[rc, a2, a4] + ([st_prev] if ob2 == 0 else []))
                        exq2[q] = no
                        lg2 = ek.op(VEC, lambda eng, ob2=ob2, asm=asm: eng.tensor_copy(
                            out=orow3[:, ob2, 64:65], in_=asm[:, 6:7]), deps=[tlg])
                        zz = ek.op(VEC, lambda eng, ob2=ob2: eng.tensor_copy(
                            out=orow3[:, ob2, 65:68], in_=c_zero4.bitcast(F16)[:, 0:3]),
                            deps=())
                        pfree[pbuf] = no
                        if j == 0:
                            pfree[1 - pbuf] = no
                        if j == 7:
                            st_prev = st = ek.dma(SCA, osort_d.rearrange(
                                "(a blk p) w -> a p blk w", p=128, blk=8)
                                [r * (NCH // 8) + pb],
                                a_orow.rearrange("p (blk w) -> p blk w", w=OROW),
                                ch="ost", deps=[no, lg2, zz])
            ek.barrier()
            if dev:
                ek.dma(SYN, taps["osrt"][:, :], osort_d[:, :], ch="tap")
                ek.barrier()

            if UPTO == "A4":
                ek.barrier()
                continue
            g3 = a_gath.rearrange("p (n r w) -> p n r w", r=NR, w=OROW)
            hb_prev = None
            gfree = [None, None]
            for g in range(G):
                gb = g % 2
                gts = []
                for r in range(NR):
                    gt = ek.idma(g3[:, gb, r, :], osort_d, ch=f"bw{gb}",
                                 deps=[gfree[gb]] if r == 0 else [],
                                 in_offset=IOA(ap=destU3[:, r, g:g + 1], axis=0),
                                 element_offset=r * S * OROW)
                    gts.append(gt)
                lcols = g3[:, gb, :, 64:65].squeeze(2)
                tlm = ek.op(VEC, lambda eng, lcols=lcols: eng.tensor_reduce(
                    a_csm[:, 4:5], lcols, axis=AX.X, op=ALU.max), deps=[gts[-1]])
                tng = ek.op(VEC, lambda eng: eng.tensor_scalar(
                    a_csm[:, 5:6], a_csm[:, 4:5], -1.0, None, op0=ALU.mult), deps=[tlm])
                tpw = ek.op(SCA, lambda eng, lcols=lcols: eng.activation(
                    a_csm[:, 0:4], lcols, AF.Exp, bias=a_csm[:, 5:6],
                    accum_out=a_csm[:, 6:7]), deps=[tng])
                trc2 = ek.op(VEC, lambda eng: eng.reciprocal(
                    a_csm[:, 7:8], a_csm[:, 6:7]), deps=[tpw])
                tw = ek.op(VEC, lambda eng: eng.tensor_scalar(
                    a_csm[:, 0:4], a_csm[:, 0:4], a_csm[:, 7:8], None,
                    op0=ALU.mult), deps=[trc2])
                acc = a_ptile.bitcast(F32)[:, 0:64]
                tacc = ek.op(VEC, lambda eng, gb=gb: eng.tensor_scalar(
                    acc, g3[:, gb, 0, 0:64], a_csm[:, 0:1], None, op0=ALU.mult),
                    deps=[tw])
                for r in range(1, NR):
                    tacc = ek.op(VEC, lambda eng, gb=gb, r=r: eng.scalar_tensor_tensor(
                        acc, g3[:, gb, r, 0:64], a_csm[:, r:r + 1], acc,
                        op0=ALU.mult, op1=ALU.add), deps=[tacc])
                th = ek.op(VEC, lambda eng: eng.tensor_copy(
                    out=a_hrow[:, 0:64], in_=acc), deps=[tacc, hb_prev])
                gfree[gb] = th
                hb_prev = ek.dma(SCA, heads_dr[g * 128:(g + 1) * 128,
                                               h2 * 64:(h2 + 1) * 64],
                                 a_hrow[:, 0:64], ch="hd", deps=[th])
            ek.barrier()
        if dev:
            ek.dma(SYN, taps["heads"][:, :], heads_dr[:, :], ch="tap")
        if phase == "A":
            ek.dma(SYN, ho_out[b][:, :], heads_dr[:, :], ch="ho")
        ek.barrier()
        ar.pop()
        return dict()

    def emit_F(b):
        if cfg.get("upto", "full") not in ("full", "F1", "F"):
            return
        if phase == "A":
            return
        ar.push()
        f_attnT = ar.take(2 * 128 // 2, BF16, align=8)
        f_woc = ar.take(D // 2, BF16) if phase != "F" else None
        f_wof = None
        f_x1T = ar.take(8 * min(1024, TOK) // 2, BF16, align=8)
        f_h = ar.take(32 * min(1024, TOK) // 2, BF16)
        f_h2 = ar.take(32 * min(1024, TOK) // 2, BF16)
        f_wp = ar.take(2 * 32 * 128 // 2, BF16)
        f_xf = ar.take(D)
        f_pp = ar.take(2 * D)
        f_r = ar.take(D)
        f_x1s = ar.take(D // 2, BF16)
        f_sm = ar.take(16)
        f_y = ar.take(D)
        f_z = f_y.bitcast(BF16)
        TSL = min(1024, TOK)
        NSL = TOK // TSL
        at4 = f_attnT.rearrange("p (n t) -> p n 1 t", n=2) if False else f_attnT.rearrange("p (n t) -> p n t", n=2)
        x1T3 = f_x1T.rearrange("p (n t) -> p n t", t=TSL)
        h3 = f_h.rearrange("p (n t) -> p n t", t=TSL)
        h23 = f_h2.rearrange("p (n t) -> p n t", t=TSL)
        wp3 = f_wp.rearrange("p (n k c) -> p n k c", n=2, c=128)
        yT3 = x1T3
        xf2 = f_xf.unsqueeze(1).transpose([0, 2, 1]) if False else f_xf.rearrange("p (n d) -> p n d", d=D, n=1)
        fpp2 = f_pp.rearrange("p (n d) -> p n d", d=D)
        x1s2 = f_x1s.rearrange("p (n d) -> p n d", d=D, n=1)

        if phase == "F":
            wof3 = f_h[:, 0:8 * D].rearrange("p (k c) -> p k c", c=D)
            two = ek.dma(GPS, wof3, woutF_d.rearrange("(k p) c -> p k c", p=128),
                         ch="wld")
        else:
            two = ek.dma(GPS, f_woc, woutT_d[:, :], ch="wld")
        atu = [None, None]
        ppu_prev = None
        st_pp = None
        for tt in range(0 if phase == "F" else S // 128):
            bb = tt % 2
            tat = ek.dma(SYN, at4[:, bb, :],
                         heads_dr[tt * 128:(tt + 1) * 128, :],
                         ch=f"at{bb}", deps=[atu[bb]], transpose=True)
            mmt = None
            for dh in range(2):
                mmt = ek.op(TENS, lambda eng, bb=bb, dh=dh: eng.matmul(
                    (p_a if dh == 0 else p_b)[:, 0:512],
                    at4[:, bb, :], f_woc[:, dh * 512:(dh + 1) * 512],
                    start=True, stop=True),
                    deps=[tat, two, ppu_prev] if dh == 0 else ())
            atu[bb] = mmt
            e1 = ek.op(VEC, lambda eng, bb=bb: eng.tensor_copy(
                out=fpp2[:, bb, 0:512], in_=p_a[:, 0:512]), deps=[mmt, st_pp])
            e2 = ek.op(VEC, lambda eng, bb=bb: eng.tensor_copy(
                out=fpp2[:, bb, 512:1024], in_=p_b[:, 0:512]), deps=[mmt])
            ppu_prev = e2
            st_pp = ek.dma(SCA, part_dr[tt * 128:(tt + 1) * 128, :],
                           fpp2[:, bb, :], ch=f"pp{bb}", deps=[e1, e2])
        ek.barrier()
        if phase != "F":
            if cfg.get("no_cc"):
                tcc = ek.dma(GPS, rs_dr[:, :], part_dr[0:TOK, :], ch="cc")
            else:
                tcc = ek.op(GPS, lambda eng: eng.collective_compute(
                    "ReduceScatter", ALU.add, [list(range(NC))],
                    [part_dr[:, :]], [rs_dr[:, :]]), n=16)
                ek.wait(GPS, tcc)
            ek.barrier()

        sproj = []
        xfu_prev = None
        x1u_prev = None
        fr_prev = None
        rsu = [None, None]
        for tt in range(TOK // 128):
            bb = tt % 2
            txf = ek.dma(SYN, xf2[:, 0, :],
                         xf_in[b][tt * 128:(tt + 1) * 128, :], ch="xf0",
                         deps=[xfu_prev])
            if phase == "F":
                mm_buf = [atu[0], atu[1]]
                mmt = None
                for rk in range(8):
                    cb = rk % 2
                    tld = ek.dma(SYN, at4[:, cb, :],
                                 htok_in[b][tt * 128:(tt + 1) * 128,
                                            rk * 128:(rk + 1) * 128],
                                 ch=f"at{cb}", deps=[mm_buf[cb]], transpose=True)
                    for dh in range(2):
                        mmt = ek.op(TENS, lambda eng, rk=rk, cb=cb, dh=dh: eng.matmul(
                            (p_a if dh == 0 else p_b)[:, 0:512],
                            at4[:, cb, :], wof3[:, rk, dh * 512:(dh + 1) * 512],
                            start=(rk == 0), stop=(rk == 7)),
                            deps=[tld, two, rsu[bb], sproj[-1] if sproj else None]
                            if rk == 0 and dh == 0 else [tld])
                    mm_buf[cb] = mmt
                atu[0], atu[1] = mm_buf
                e1 = ek.op(VEC, lambda eng, bb=bb: eng.tensor_copy(
                    out=fpp2[:, bb, 0:512], in_=p_a[:, 0:512]), deps=[mmt])
                trs0 = ek.op(VEC, lambda eng, bb=bb: eng.tensor_copy(
                    out=fpp2[:, bb, 512:1024], in_=p_b[:, 0:512]), deps=[mmt])
            else:
                trs0 = ek.dma(SYN, fpp2[:, bb, :],
                              rs_dr[tt * 128:(tt + 1) * 128, :], ch=f"rs{bb}",
                              deps=[rsu[bb]])
            t1 = ek.op(VEC, lambda eng, bb=bb: eng.tensor_tensor(
                f_r[:, 0:D], fpp2[:, bb, :], xf2[:, 0, :], op=ALU.add),
                deps=[txf, trs0, sproj[-1] if sproj else None, fr_prev])
            t2 = t1
            xfu_prev = t1
            rsu[bb] = t1
            sproj.append(t1)
            ts1 = ek.op(SCA, lambda eng: eng.activation(
                f_y[:, 0:D], f_r[:, 0:D], AF.Copy, accum_out=f_sm[:, 0:1]),
                deps=[t1, t2])
            ts2 = ek.op(SCA, lambda eng: eng.activation(
                f_y[:, 0:D], f_r[:, 0:D], AF.Square, accum_out=f_sm[:, 1:2]),
                deps=[ts1])
            tmu = ek.op(VEC, lambda eng: eng.tensor_scalar(
                f_sm[:, 2:3], f_sm[:, 0:1], 1.0 / D, None, op0=ALU.mult), deps=[ts1])
            tm2 = ek.op(VEC, lambda eng: eng.tensor_scalar(
                f_sm[:, 3:4], f_sm[:, 2:3], None, None, op0=ALU.mult,
                scalar1=None), deps=[tmu]) if False else None
            tm2 = ek.op(VEC, lambda eng: eng.tensor_tensor(
                f_sm[:, 3:4], f_sm[:, 2:3], f_sm[:, 2:3], op=ALU.mult), deps=[tmu])
            tvar = ek.op(VEC, lambda eng: eng.tensor_scalar(
                f_sm[:, 4:5], f_sm[:, 1:2], 1.0 / D, None, op0=ALU.mult), deps=[ts2])
            tvar2 = ek.op(VEC, lambda eng: eng.tensor_tensor(
                f_sm[:, 5:6], f_sm[:, 4:5], f_sm[:, 3:4], op=ALU.subtract),
                deps=[tvar, tm2])
            tsd = ek.op(SCA, lambda eng: eng.activation(
                f_sm[:, 6:7], f_sm[:, 5:6], AF.Sqrt, bias=c_zero4[:, 5:6]),
                deps=[tvar2])
            trs = ek.op(VEC, lambda eng: eng.reciprocal(
                f_sm[:, 7:8], f_sm[:, 6:7]), deps=[tsd])
            tx1 = ek.op(VEC, lambda eng, bb=bb: eng.tensor_scalar(
                x1s2[:, 0, :], f_r[:, 0:D], f_sm[:, 2:3], f_sm[:, 7:8],
                op0=ALU.subtract, op1=ALU.mult), deps=[trs, x1u_prev])
            fr_prev = tx1
            tst1 = ek.dma(SCA, x1_dr[tt * 128:(tt + 1) * 128, :], x1s2[:, 0, :],
                          ch="x10", deps=[tx1])
            x1u_prev = tst1
        ek.barrier()
        if dev:
            ek.dma(GPS, taps["x1"][:, :], x1_dr[:, :], ch="tap")
        if cfg.get("upto", "full") == "F1":
            ek.barrier()
            ar.pop()
            return

        smat_s1 = c_smat[:, 0:32]
        smat_s2 = c_smat[:, 32:64]
        smat_h1 = c_smat[:, 64:96]
        smat_h2 = c_smat[:, 96:128]
        smat_s3 = c_smat[:, 128:136]
        NT2 = TSL // 512
        for sl in range(NSL):
            t0 = sl * TSL
            tx1T = []
            for k in range(8):
                tx1T.append(ek.dma(SYN, x1T3[:, k, :],
                                   x1_dr[t0:t0 + TSL, k * 128:(k + 1) * 128],
                                   ch="x1T", transpose=True))
            wpu = [None, None]
            hdone = []
            for hm in range(32):
                wb = hm % 2
                tw1 = ek.dma(GPS, wp3[:, wb, 0:8, :],
                             w1T_d.rearrange("(k p) c -> p k c", p=128)
                             [:, :, hm * 128:(hm + 1) * 128], ch=f"w1{wb}",
                             deps=[wpu[wb]])
                mm1 = None
                for k in range(8):
                    for ts2_ in range(NT2):
                        mm1 = ek.op(TENS, lambda eng, k=k, ts2_=ts2_, wb=wb: eng.matmul(
                            (p_c if ts2_ == 0 else p_d)[:, 0:512],
                            wp3[:, wb, k, :], x1T3[:, k, ts2_ * 512:(ts2_ + 1) * 512],
                            start=(k == 0), stop=(k == 7)),
                            deps=[tw1, tx1T[-1], hdone[-1] if hdone else None]
                            if k == 0 and ts2_ == 0 else ())
                z1 = []
                for ts2_ in range(NT2):
                    z1.append(ek.op(SCA, lambda eng, hm=hm, ts2_=ts2_: eng.activation(
                        f_z[:, ts2_ * 512:(ts2_ + 1) * 512],
                        (p_c if ts2_ == 0 else p_d)[:, 0:512], AF.Silu,
                        scale=smat_s1[:, hm:hm + 1]), deps=[mm1]))
                tw2 = ek.dma(GPS, wp3[:, wb, 8:16, :],
                             w2T_d.rearrange("(k p) c -> p k c", p=128)
                             [:, :, hm * 128:(hm + 1) * 128], ch=f"w2{wb}",
                             deps=[wpu[wb]])
                mm2 = None
                for k in range(8):
                    for ts2_ in range(NT2):
                        mm2 = ek.op(TENS, lambda eng, k=k, ts2_=ts2_, wb=wb: eng.matmul(
                            (p_c if ts2_ == 0 else p_d)[:, 0:512],
                            wp3[:, wb, 8 + k, :], x1T3[:, k, ts2_ * 512:(ts2_ + 1) * 512],
                            start=(k == 0), stop=(k == 7)),
                            deps=[tw2] + z1 if k == 0 and ts2_ == 0 else ())
                z2 = []
                for ts2_ in range(NT2):
                    z2.append(ek.op(SCA, lambda eng, hm=hm, ts2_=ts2_: eng.activation(
                        f_z[:, 1024 + ts2_ * 512:1024 + (ts2_ + 1) * 512],
                        (p_c if ts2_ == 0 else p_d)[:, 0:512], AF.Copy,
                        scale=smat_s2[:, hm:hm + 1]), deps=[mm2]))
                th = ek.op(VEC, lambda eng, hm=hm: eng.tensor_tensor(
                    h3[:, hm, :], f_z[:, 0:TSL], f_z[:, 1024:1024 + TSL],
                    op=ALU.mult), deps=z1 + z2)
                hdone.append(th)
                wpu[wb] = mm2
            ek.barrier()
            sdone_prev = hdone[-1]
            for stage, (wt_d, sc, src3, dst3) in enumerate(
                    ((h1T_d, smat_h1, h3, h23), (h2T_d, smat_h2, h23, h3))):
                if stage == 1:
                    ek.barrier()
                wpu2 = [None, None]
                sdone = []
                for hm in range(32):
                    wb = hm % 2
                    twp = ek.dma(GPS, wp3[:, wb, :, :],
                                 wt_d.rearrange("(k p) c -> p k c", p=128)
                                 [:, :, hm * 128:(hm + 1) * 128], ch=f"h{stage}{wb}",
                                 deps=[wpu2[wb]])
                    mmh = None
                    for k in range(32):
                        for ts2_ in range(NT2):
                            mmh = ek.op(TENS, lambda eng, k=k, ts2_=ts2_, wb=wb, src3=src3: eng.matmul(
                                (p_c if ts2_ == 0 else p_d)[:, 0:512],
                                wp3[:, wb, k, :], src3[:, k, ts2_ * 512:(ts2_ + 1) * 512],
                                start=(k == 0), stop=(k == 31)),
                                deps=[twp, sdone_prev,
                                      sdone[-1] if sdone else None]
                                if k == 0 and ts2_ == 0 else ())
                    gouts = []
                    for ts2_ in range(NT2):
                        gouts.append(ek.op(SCA, lambda eng, hm=hm, ts2_=ts2_, dst3=dst3, sc=sc: eng.activation(
                            dst3[:, hm, ts2_ * 512:(ts2_ + 1) * 512],
                            (p_c if ts2_ == 0 else p_d)[:, 0:512], AF.Gelu,
                            scale=sc[:, hm:hm + 1]), deps=[mmh]))
                    sdone.append(gouts[-1])
                    wpu2[wb] = gouts[-1]
                sdone_prev = sdone[-1]
            ek.barrier()
            w3u = [None, None]
            ydone = []
            for dm in range(8):
                wb = dm % 2
                tw3 = ek.dma(GPS, wp3[:, wb, :, :],
                             w3T_d.rearrange("(k p) c -> p k c", p=128)
                             [:, :, dm * 128:(dm + 1) * 128], ch=f"w3{wb}",
                             deps=[w3u[wb]])
                mm3 = None
                for k in range(32):
                    for ts2_ in range(NT2):
                        mm3 = ek.op(TENS, lambda eng, k=k, ts2_=ts2_, wb=wb: eng.matmul(
                            (p_c if ts2_ == 0 else p_d)[:, 0:512],
                            wp3[:, wb, k, :], h3[:, k, ts2_ * 512:(ts2_ + 1) * 512],
                            start=(k == 0), stop=(k == 31)),
                            deps=[tw3, sdone_prev, ydone[-1] if ydone else None]
                            if k == 0 and ts2_ == 0 else ())
                for ts2_ in range(NT2):
                    ydone.append(ek.op(SCA, lambda eng, dm=dm, ts2_=ts2_: eng.activation(
                        yT3[:, dm, ts2_ * 512:(ts2_ + 1) * 512],
                        (p_c if ts2_ == 0 else p_d)[:, 0:512], AF.Copy,
                        scale=smat_s3[:, dm:dm + 1]), deps=[mm3]))
                w3u[wb] = ydone[-1]
            ek.barrier()
            x1ld_prev = None
            x1lu_prev = None
            ptk = None
            for t128 in range(TSL // 128):
                bb = t128 % 2
                gt = t0 + t128 * 128
                x1ld_prev = ek.dma(SYN, x1s2[:, 0, :],
                                   x1_dr[gt:gt + 128, :], ch="x1r",
                                   deps=[x1lu_prev])
                radd = []
                for dk in range(8):
                    tp = ek.op(TENS, lambda eng, dk=dk, t128=t128: eng.transpose(
                        p_t.bitcast(BF16)[:, 0:128],
                        yT3[:, dk, t128 * 128:(t128 + 1) * 128], c_identb),
                        deps=[ydone[-1], ptk])
                    ptk = ek.op(VEC, lambda eng, dk=dk: eng.tensor_copy(
                        out=f_y[:, dk * 128:(dk + 1) * 128],
                        in_=p_t.bitcast(BF16)[:, 0:128]), deps=[tp])
                    radd.append(ptk)
                t1 = ek.op(VEC, lambda eng, bb=bb: eng.tensor_tensor(
                    f_r[:, 0:D], f_y[:, 0:D], x1s2[:, 0, :], op=ALU.add),
                    deps=radd + [x1ld_prev])
                x1lu_prev = t1
                ts1 = ek.op(SCA, lambda eng: eng.activation(
                    f_y[:, 0:D], f_r[:, 0:D], AF.Copy, accum_out=f_sm[:, 0:1]),
                    deps=[t1] + radd)
                ts2b = ek.op(SCA, lambda eng: eng.activation(
                    f_y[:, 0:D], f_r[:, 0:D], AF.Square, accum_out=f_sm[:, 1:2]),
                    deps=[ts1])
                tmu = ek.op(VEC, lambda eng: eng.tensor_scalar(
                    f_sm[:, 2:3], f_sm[:, 0:1], 1.0 / D, None, op0=ALU.mult), deps=[ts1])
                tm2 = ek.op(VEC, lambda eng: eng.tensor_tensor(
                    f_sm[:, 3:4], f_sm[:, 2:3], f_sm[:, 2:3], op=ALU.mult), deps=[tmu])
                tvar = ek.op(VEC, lambda eng: eng.tensor_scalar(
                    f_sm[:, 4:5], f_sm[:, 1:2], 1.0 / D, None, op0=ALU.mult), deps=[ts2b])
                tvar2 = ek.op(VEC, lambda eng: eng.tensor_tensor(
                    f_sm[:, 5:6], f_sm[:, 4:5], f_sm[:, 3:4], op=ALU.subtract),
                    deps=[tvar, tm2])
                tsd = ek.op(SCA, lambda eng: eng.activation(
                    f_sm[:, 6:7], f_sm[:, 5:6], AF.Sqrt, bias=c_zero4[:, 5:6]),
                    deps=[tvar2])
                trs = ek.op(VEC, lambda eng: eng.reciprocal(
                    f_sm[:, 7:8], f_sm[:, 6:7]), deps=[tsd])
                tyo = ek.op(VEC, lambda eng: eng.tensor_scalar(
                    f_r[:, 0:D], f_r[:, 0:D], f_sm[:, 2:3], f_sm[:, 7:8],
                    op0=ALU.subtract, op1=ALU.mult), deps=[trs, ts2b])
                tso = ek.dma(SCA, y_out[b][gt:gt + 128, :], f_r[:, 0:D],
                             ch="yo", deps=[tyo])
                ptk = tso
            ek.barrier()
        ek.barrier()
        ar.pop()

    for b in range(min(NB, 1) if dev else NB):
        emit_A(b)
        emit_F(b)

    blk = enter(nc.Block())
    ek.finalize(blk)
    for h in reversed(ctx):
        h.__exit__(None, None, None)
    return nc, taps


def _host_constants(cfg):
    S, D, NR, NBK, G = cfg["S"], cfg["D"], cfg["NR"], cfg["NBK"], cfg["G"]
    W = 2 * NBK
    c = {}
    c["iotaK"] = np.tile(np.arange(NBK, dtype=np.float32), (128, 1))
    c["iotaT"] = np.tile(np.arange(W, dtype=np.float16), (128, 2))
    c["ltri"] = np.triu(np.ones((128, 128), np.float32), 1).astype(NPBF)
    st = np.zeros((128, 256), np.float32)
    st[:, 128:] = 1.0
    c["stair"] = st.astype(NPBF)
    c["onesc"] = np.ones((128, 1), NPBF)
    c["ones1"] = np.ones((1, 128), np.float32)
    c["identb"] = np.eye(128, dtype=np.float32).astype(NPBF)
    c["identh"] = np.eye(128, dtype=np.float32).astype(np.float16)
    ih = np.zeros((128, 64), np.float16)
    ih[64:, :] = np.eye(64, dtype=np.float16)
    c["identhh"] = ih
    c["identb2"] = np.tile(np.eye(64, dtype=np.float32), (2, 1)).astype(NPBF)
    z = np.zeros((128, 8), np.float32)
    z[:, 4] = 1e-6
    z[:, 5] = 1e-5
    c["zero4"] = z
    return c


def _prep_core_inputs(cfg, inputs, core):
    S, D, NR, NBK = cfg["S"], cfg["D"], cfg["NR"], cfg["NBK"]
    NC, TOK, NB = cfg["NC"], cfg["TOK"], cfg["NB"]
    HD = 4 * D
    import jax
    x = np.ascontiguousarray(np.asarray(inputs["L0_attn_x"], np.float32))
    h0 = 2 * core
    m = {}
    for b in range(NB):
        m[f"x{b}"] = x[b]
        m[f"xf{b}"] = np.ascontiguousarray(x[b][core * TOK:(core + 1) * TOK])
    wqk = np.asarray(inputs["L0_attn_w_qk"], np.float32)
    wv = np.asarray(inputs["L0_attn_w_v"], np.float32)
    wout = np.asarray(inputs["L0_attn_w_out"], np.float32)
    m["wqkT"] = np.ascontiguousarray(wqk[h0 * 64:(h0 + 2) * 64, :].T)
    m["wvT"] = np.ascontiguousarray(wv[h0 * 64:(h0 + 2) * 64, :].T)
    m["woutT"] = np.ascontiguousarray(wout.T[core * 128:(core + 1) * 128, :])
    rot = _rot_table(cfg)
    m["rot2"] = np.ascontiguousarray(
        rot[h0:h0 + 2].reshape(2, 64, NR * NBK).reshape(128, NR * NBK))
    m["w1T"] = np.ascontiguousarray(np.asarray(inputs["L0_ffn_w1_w"], np.float32).T)
    m["w2T"] = np.ascontiguousarray(np.asarray(inputs["L0_ffn_w2_w"], np.float32).T)
    m["h1T"] = np.ascontiguousarray(np.asarray(inputs["L0_ffn_h1_w"], np.float32).T)
    m["h2T"] = np.ascontiguousarray(np.asarray(inputs["L0_ffn_h2_w"], np.float32).T)
    m["w3T"] = np.ascontiguousarray(np.asarray(inputs["L0_ffn_w3_w"], np.float32).T)
    sm = np.zeros((128, 4 * (HD // 128) + D // 128), np.float32)
    col = 0
    for nm in ("L0_ffn_w1_s", "L0_ffn_w2_s", "L0_ffn_h1_s", "L0_ffn_h2_s"):
        v = np.asarray(inputs[nm], np.float32).reshape(HD // 128, 128).T
        sm[:, col:col + HD // 128] = v
        col += HD // 128
    v = np.asarray(inputs["L0_ffn_w3_s"], np.float32).reshape(D // 128, 128).T
    sm[:, col:col + D // 128] = v
    m["smat"] = sm
    m.update(_host_constants(cfg))
    return m


_ROT_CACHE = {}
def _rot_table(cfg):
    key = (cfg["S"], cfg["NR"], cfg["NBK"])
    if key not in _ROT_CACHE:
        import jax
        with jax.default_device(jax.devices("cpu")[0]):
            r = jax.random.normal(jax.random.key(42),
                                  (16, 64, cfg["NR"], cfg["NBK"]), dtype="float32")
            _ROT_CACHE[key] = np.asarray(r)
    return _ROT_CACHE[key]


def kernel(**inputs):
    from concourse.library_overlay import lower_extended_insts
    cfg = make_cfg()
    NC, NB, TOK, D, S = cfg["NC"], cfg["NB"], cfg["TOK"], cfg["D"], cfg["S"]
    in_maps = [_prep_core_inputs(cfg, inputs, c) for c in range(NC)]
    cfgA = dict(cfg); cfgA["phase"] = "A"
    ncA, _ = build_program(cfgA)
    lower_extended_insts(ncA)
    resA = run_bass_kernel_spmd(ncA, in_maps, core_ids=list(range(NC)))
    heads = [np.concatenate([np.asarray(resA.results[c][f"ho{b}"]) for c in range(NC)],
                            axis=1) for b in range(NB)]
    cfgF = dict(cfg); cfgF["phase"] = "F"
    ncF, _ = build_program(cfgF)
    lower_extended_insts(ncF)
    wout = np.asarray(inputs["L0_attn_w_out"], np.float32)
    for c in range(NC):
        for b in range(NB):
            in_maps[c][f"htok{b}"] = np.ascontiguousarray(heads[b][c * TOK:(c + 1) * TOK])
        in_maps[c]["woutF"] = np.ascontiguousarray(wout.T)
    resF = run_bass_kernel_spmd(ncF, in_maps, core_ids=list(range(NC)))
    out = np.zeros((NB, S, D), np.float32)
    for c in range(NC):
        for b in range(NB):
            out[b, c * TOK:(c + 1) * TOK] = resF.results[c][f"y{b}"]
    return out
